# version 20
# baseline (speedup 1.0000x reference)
import math
from contextlib import ExitStack

import numpy as np
import concourse.bass as bass
import concourse.mybir as mybir
from concourse.alu_op_type import AluOpType
from concourse.bass_utils import run_bass_kernel_spmd

F32 = mybir.dt.float32
BF16 = mybir.dt.bfloat16
AF = mybir.ActivationFunctionType
AX = mybir.AxisListType.X

B, S, D = 4, 2048, 4096
HL, HD = 16, 128
TOWN = 1024
NT = TOWN // 128
KEXP, RANK = 64, 16
KR = KEXP * RANK
QSCALE = HD ** (-0.5)
EPS = 1e-5
NEG = -1.0e9

ENGS = ("tensor", "vector", "scalar", "sync", "gpsimd")
CC, TD, VD, SD = "cc", "td", "vd", "sd"
SYNC_SEMS = ("ly0", "ly1", "ly2", "ly3")
GP_SEMS = ("lg0", "lg1")
ALL_SEMS = SYNC_SEMS + GP_SEMS + (CC, TD, VD, SD)


def get_alibi_slopes(n):
    cp2 = 2 ** math.floor(math.log2(n))
    base = 2 ** (-(2 ** (-(math.log2(cp2) - 3))))
    powers = np.arange(1, 1 + cp2, dtype=np.float32)
    slopes = np.power(base, powers).astype(np.float32)
    if cp2 != n:
        eb = 2 ** (-(2 ** (-(math.log2(2 * cp2) - 3))))
        ep = np.arange(1, 1 + 2 * min(cp2, n - cp2), 2, dtype=np.float32)
        slopes = np.concatenate([slopes, np.power(eb, ep).astype(np.float32)])
    return slopes


class Plan:
    def __init__(self):
        self.ops = {k: [] for k in ENGS}
        self.cnt = {}

    def do(self, eng, fn, waits=(), inc=None, n=1):
        if inc is not None:
            self.cnt[inc] = self.cnt.get(inc, 0) + n
        self.ops[eng].append((tuple(waits), fn, inc, n))
        return self.cnt.get(inc, 0) if inc is not None else None

    def v(self, sem):
        return self.cnt.get(sem, 0)

    def barrier(self):
        snap = tuple((k, v) for k, v in self.cnt.items() if v > 0)
        for eng in ENGS:
            self.ops[eng].append((snap, None, None, 0))


WIN_MARGIN = 164.0


def local_windows():
    slopes = get_alibi_slopes(32)
    wins = []
    for i in range(HL):
        sl = min(slopes[2 * i], slopes[2 * i + 1])
        w = int(math.ceil(WIN_MARGIN / sl))
        wins.append(min(w, S))
    return wins


def build(dbg=False):
    nc = bass.Bass("TRN2", target_bir_lowering=False)
    WINS = local_windows()

    def inp(name, shape):
        return nc.dram_tensor(name, shape, F32, kind="ExternalInput").ap()

    x_in = inp("x", [TOWN, D])
    wqkv = inp("wqkv", [3, HL, D, HD])
    wout = inp("wout", [32, D, 128])
    a_fm = inp("a_fm", [D, KR])
    b_t = inp("b_t", [KR, D])
    wr_fm = inp("wr_fm", [D, KEXP])
    bq_col = inp("bq_col", [128, HL])
    bk_col = inp("bk_col", [128, HL])
    bv_col = inp("bv_col", [128, HL])
    bo_col = inp("bo_col", [128, 32])
    ident = inp("ident", [128, 128])
    tri = inp("tri", [128, 128])
    alk3 = inp("alk3", [3, HL, S])
    ones3 = inp("ones3", [3, 128])
    msel = inp("msel", [128, 1])

    out_ext = nc.dram_tensor("out", [TOWN, D], F32, kind="ExternalOutput").ap()

    def scratch(name, shape, dt):
        kind = "ExternalOutput" if dbg else "Internal"
        return nc.dram_tensor(name, shape, dt, kind=kind).ap()

    sc_x1 = scratch("sc_x1", [TOWN, D], F32)
    sc_x2 = scratch("sc_x2", [TOWN, D], F32)
    sc_q = scratch("sc_q", [HL, 128, S], BF16)
    sc_k = scratch("sc_k", [HL, 128, S], BF16)
    sc_v = scratch("sc_v", [16, 128, HL, 128], BF16)
    agin = [nc.dram_tensor("agin0", [1, 32, 128, 512], BF16).ap(),
            nc.dram_tensor("agin1", [1, 32, 128, 512], BF16).ap()]
    agout = [nc.dram_tensor("agout0", [2, 32, 128, 512], BF16).ap(),
             nc.dram_tensor("agout1", [2, 32, 128, 512], BF16).ap()]
    cin = [nc.dram_tensor("cin0", [1, 8, 128, S], BF16).ap(),
           nc.dram_tensor("cin1", [1, 8, 128, S], BF16).ap()]
    cout = [nc.dram_tensor("cout0", [2, 8, 128, S], BF16).ap(),
            nc.dram_tensor("cout1", [2, 8, 128, S], BF16).ap()]
    groups = [[0, 1], [2, 3], [4, 5], [6, 7]]

    p = Plan()

    with ExitStack() as ctx:
        def sb(name, shape, dt):
            return ctx.enter_context(nc.sbuf_tensor(name, shape, dt))

        sems = {}
        for s_ in ALL_SEMS:
            sems[s_] = ctx.enter_context(nc.semaphore(s_))

        ident_sb = sb("ident_sb", [128, 128], BF16)
        tri_sb = sb("tri_sb", [128, 128], F32)
        ones3_sb = sb("ones3_sb", [3, 128], BF16)
        alk_sb = sb("alk_sb", [3, S], BF16)
        bq_sb = sb("bq_sb", [128, HL], F32)
        bk_sb = sb("bk_sb", [128, HL], F32)
        bv_sb = sb("bv_sb", [128, HL], F32)
        bo_sb = sb("bo_sb", [128, 32], F32)
        bqs_sb = sb("bqs_sb", [128, HL], F32)
        msel_sb = sb("msel_sb", [128, 1], F32)

        bnst = sb("bnst", [128, 48], F32)
        mv = sb("mv", [128, 2], F32)
        t1 = sb("t1", [128, 1], F32)
        rstd = sb("rstd", [128, 1], F32)
        negm = sb("negm", [128, 1], F32)
        negm2 = sb("negm2", [128, 2], F32)
        zacc = sb("zacc", [128, 1], F32)
        rz1 = sb("rz1", [128, 1], F32)
        scr = sb("scr", [128, KEXP], F32)
        top8a = sb("top8a", [128, 8], F32)
        top8b = sb("top8b", [128, 8], F32)
        rep = sb("rep", [128, KEXP], F32)
        e_r = sb("e_r", [128, KEXP], F32)
        selm = sb("selm", [128, KEXP], F32)
        eg = sb("eg", [128, KEXP], F32)
        zg = sb("zg", [128, 1], F32)
        rz_sb = sb("rz_sb", [128, NT], F32)

        ar1 = sb("ar1", [128, 32768], BF16)
        ar2 = sb("ar2", [128, 32768], BF16)
        pool = sb("pool", [128, 16384], BF16)
        wr_sb = sb("wr_sb", [128, 32, KEXP], BF16)
        xn_bf = sb("xn_bf", [128, D], BF16)
        hp_bf = sb("hp_bf", [128, KR], BF16)
        hp_bf2 = sb("hp_bf2", [128, KR], BF16)

        a_v = ar1[:].rearrange("p (i k) -> p i k", i=32)
        bt_v = ar1[:].rearrange("p (j d) -> p j d", j=8)
        ao_v = ar1[:].rearrange("p (t f) -> p t f", t=8)
        xn1a_v = ar1[:].rearrange("p (i t) -> p i t", i=16)
        xn1b_v = ar2[:].rearrange("p (i t) -> p i t", i=16)
        ctx_v = ar2[:].rearrange("p (i t) -> p i t", i=32)

        ar2f = ar2[:].bitcast(F32)
        hpf_v = ar2[:, 0:8192].rearrange("p (j t) -> p j t", j=8)
        xnf2 = [ar2[:, 8192:12288].rearrange("p (i t) -> p i t", i=32),
                ar2[:, 12288:16384].rearrange("p (i t) -> p i t", i=32)]
        xbuf2 = [ar2f[:, 8192:12288], ar2f[:, 12288:16384]]

        poolf = pool[:].bitcast(F32)
        xbuf3 = poolf[:, 0:4096]
        w_buf = [pool[:, 0:4096].rearrange("p (i f) -> p i f", i=32),
                 pool[:, 4096:8192].rearrange("p (i f) -> p i f", i=32)]
        qe_buf = [pool[:, 8192:10240], pool[:, 10240:12288]]
        v_buf = pool[:, 12288:14336].rearrange("p (c f) -> p c f", c=16)
        k_sb2 = [ar2[:, 0:2048], ar2[:, 2048:4096]]
        q_sb2 = [ar2[:, 4096:6144], ar2[:, 6144:8192]]
        v_ts2 = [ar2[:, 8192:10240].rearrange("p (c f) -> p c f", c=16),
                 ar2[:, 10240:12288].rearrange("p (c f) -> p c f", c=16)]
        ctxh_sb2 = [ar2[:, 12288:14336], ar2[:, 14336:16384]]
        e_sb2 = [ar2[:, 16384:18432], ar2[:, 18432:20480]]
        et_sb2 = [ar2[:, 20480:22528], ar2[:, 22528:24576]]
        bm_sb = sb("bm_sb", [128, 4], F32)
        zb4 = sb("zb4", [128, 4], F32)
        mgA = pool[:, 0:8192].rearrange("p (c t) -> p c t", c=8)
        mgB = pool[:, 8192:16384].rearrange("p (c t) -> p c t", c=8)
        aoe_buf = pool[:, 8192:9216]
        xt_p6 = poolf[:, 0:4096]

        ps_big = ctx.enter_context(nc.psum_tensor("ps_big", [128, 2048], F32))
        ps_med = ctx.enter_context(nc.psum_tensor("ps_med", [128, 1024], F32))
        ps_r = ctx.enter_context(nc.psum_tensor("ps_r", [128, 512], F32))
        ps_x = ctx.enter_context(nc.psum_tensor("ps_x", [128, 1024], BF16))
        px_med = ps_med[:, 0:512].bitcast(BF16)
        px_big3 = ps_big[:, 1536:2048].bitcast(BF16)

        mm_mark = {}
        hp_transp_mark = [0, 0]
        xn_mark = [0]
        ag_half_mark = [[]]
        cc_marks = {}

        rot = {"sync": list(SYNC_SEMS), "gpsimd": list(GP_SEMS)}
        rix = {"sync": 0, "gpsimd": 0}

        def dma(eng, _sem, out, in_, waits=()):
            i = rix[eng]
            rix[eng] = (i + 1) % len(rot[eng])
            s_ = rot[eng][i]
            wt = [(s_, p.v(s_))] + list(waits)
            c = p.do(eng, lambda e, o=out, i_=in_: e.dma_start(out=o, in_=i_),
                     waits=wt, inc=s_, n=16)
            return (s_, c)

        def fence(eng):
            return [(s_, p.v(s_)) for s_ in rot[eng]]

        def fence_all():
            return fence("sync") + fence("gpsimd")

        dma("gpsimd", None, ident_sb[:], ident)
        dma("gpsimd", None, ones3_sb[:], ones3)
        dma("sync", None, tri_sb[:], tri)
        dma("sync", None, bq_sb[:], bq_col)
        dma("sync", None, bk_sb[:], bk_col)
        dma("sync", None, bv_sb[:], bv_col)
        dma("sync", None, bo_sb[:], bo_col)
        dma("sync", None, msel_sb[:], msel)
        p.do("vector", lambda e: e.tensor_scalar(
            bqs_sb[:], bq_sb[:], QSCALE, None, AluOpType.mult),
            waits=fence("sync"), inc=VD)
        p.barrier()

        def ln_tile(x_ap, out_bf):
            p.do("vector", lambda e: e.drain())
            for i in range(8):
                p.do("vector", lambda e, i=i: e.bn_stats(
                    bnst[:, 6 * i:6 * i + 6], x_ap[:, 512 * i:512 * (i + 1)]), inc=VD)
            p.do("vector", lambda e: e.drain())
            p.do("vector", lambda e: e.bn_aggr(mv[:], bnst[:]), inc=VD)
            p.do("vector", lambda e: e.drain())
            cv = p.do("vector", lambda e: e.tensor_scalar(
                t1[:], mv[:, 1:2], EPS, None, AluOpType.add), inc=VD)
            cs = p.do("scalar", lambda e: e.activation(rstd[:], t1[:], AF.Sqrt),
                      waits=[(VD, cv)], inc=SD)
            p.do("vector", lambda e: e.reciprocal(rstd[:], rstd[:]),
                 waits=[(SD, cs)], inc=VD)
            p.do("vector", lambda e: e.drain())
            cv = p.do("vector", lambda e: e.tensor_scalar(
                out_bf[:], x_ap[:], mv[:, 0:1], rstd[:],
                AluOpType.subtract, AluOpType.mult),
                waits=[(TD, xn_mark[0])], inc=VD)
            return cv

        def transpose_128(src_bf, n128, dst_fn, waits0, copy_waits=(), evac="scalar",
                          px2=None):
            cs_pair = [p.v(SD), p.v(VD)]
            cw_done = set()
            PK = 8 if px2 is not None else 4
            rmark = {0: [p.v(SD), p.v(VD)], 1: [p.v(SD), p.v(VD)]}
            for j0 in range(0, n128, PK):
                jn = min(PK, n128 - j0)
                pk = (j0 // PK)
                reg = pk % 2 if px2 is not None else 0
                px = ps_x if reg == 0 else px2
                if evac == "alt":
                    ev = "scalar" if pk % 2 == 0 else "vector"
                else:
                    ev = evac
                for j in range(j0, j0 + jn):
                    wt = []
                    if j == j0:
                        wt = [(SD, rmark[reg][0]), (VD, rmark[reg][1])]
                    if j == 0:
                        wt = wt + list(waits0)
                    p.do("tensor", lambda e, j=j, px=px, PK=PK: e.transpose(
                        px[:, 128 * (j % PK):128 * (j % PK) + 128],
                        src_bf[:, 128 * j:128 * j + 128], ident_sb[:]),
                        waits=wt, inc=TD)
                ct = p.v(TD)
                dst = dst_fn(j0, jn)
                src = px[:, 0:128 * jn]
                if len(dst.shape) == 3:
                    src = src.rearrange("p (j f) -> p j f", j=jn)
                wt = [(TD, ct)]
                if ev not in cw_done:
                    wt += list(copy_waits)
                    cw_done.add(ev)
                if ev == "scalar":
                    cs = p.do("scalar", lambda e, dst=dst, src=src: e.copy(dst, src),
                              waits=wt, inc=SD)
                    rmark[reg][0] = cs
                    cs_pair[0] = cs
                else:
                    cs = p.do("vector", lambda e, dst=dst, src=src: e.tensor_copy(dst, src),
                              waits=wt, inc=VD)
                    rmark[reg][1] = cs
                    cs_pair[1] = cs
            return cs_pair

        def ffn_gate(t, hps, rps):
            csc = p.do("scalar", lambda e, rps=rps: e.copy(scr[:], rps),
                       waits=[(TD, mm_mark[t]), (VD, p.v(VD))], inc=SD)
            p.do("vector", lambda e: e.max(top8a[:], scr[:]),
                 waits=[(SD, csc)], inc=VD)
            p.do("vector", lambda e: e.drain())
            p.do("vector", lambda e: e.match_replace(rep[:], top8a[:], scr[:], -3.0e38), inc=VD)
            p.do("vector", lambda e: e.drain())
            p.do("vector", lambda e: e.max(top8b[:], rep[:]), inc=VD)
            p.do("vector", lambda e: e.drain())
            cv = p.do("vector", lambda e: e.tensor_scalar(
                negm[:], top8a[:, 0:1], -1.0, None, AluOpType.mult), inc=VD)
            p.do("vector", lambda e: e.drain())
            cs2 = p.do("scalar", lambda e: e.activation(
                e_r[:], scr[:], AF.Exp, bias=negm[:], scale=1.0),
                waits=[(VD, cv)], inc=SD)
            p.do("vector", lambda e: e.scalar_tensor_tensor(
                eg[:], scr[:], top8b[:, 7:8], e_r[:],
                AluOpType.is_ge, AluOpType.mult),
                waits=[(SD, cs2)], inc=VD)
            p.do("vector", lambda e: e.drain())
            p.do("vector", lambda e: e.reduce_sum(zg[:], eg[:], axis=AX), inc=VD)
            p.do("vector", lambda e: e.drain())
            p.do("vector", lambda e, t=t: e.reciprocal(rz_sb[:, t:t + 1], zg[:]), inc=VD)
            hpb = hp_bf if t % 2 == 0 else hp_bf2
            cv = p.do("vector", lambda e, hps=hps, hpb=hpb: e.tensor_tensor(
                hpb[:].rearrange("p (k r) -> p k r", k=KEXP),
                hps.rearrange("p (k r) -> p k r", k=KEXP),
                eg[:].unsqueeze(2).broadcast_to((128, KEXP, RANK)),
                AluOpType.mult), waits=[(TD, hp_transp_mark[t % 2])], inc=VD)
            return cv

        def ffn_a(x_src):
            wA = [(TD, p.v(TD)), (VD, p.v(VD)), (SD, p.v(SD))]
            la1 = dma("gpsimd", None, a_v[:, 0:16, :],
                      a_fm[0:2048, :].rearrange("(i p) k -> p i k", p=128), waits=wA)
            la2 = dma("gpsimd", None, a_v[:, 16:32, :],
                      a_fm[2048:4096, :].rearrange("(i p) k -> p i k", p=128), waits=wA)
            lg_w = dma("gpsimd", None, wr_sb[:],
                       wr_fm.rearrange("(i p) k -> p i k", p=128), waits=wA)
            hps2 = [ps_med[:, 0:1024], ps_big[:, 0:1024]]
            rps2 = [ps_r[:, 0:KEXP], ps_big[:, 1024:1024 + KEXP]]
            xb3 = [xbuf2[0], xbuf2[1], xbuf3]
            ln_mark = [0, 0, 0]
            for t in range(NT + 1):
                if t < NT:
                    pt = t % 2
                    px3 = t % 3
                    lx = dma("sync", None, xb3[px3], x_src[128 * t:128 * (t + 1), :],
                             waits=[(VD, ln_mark[px3])])
                    p.do("vector", lambda e: e.engine_nop(), waits=[lx], inc=VD)
                    cv = ln_tile(xb3[px3], xn_bf)
                    ln_mark[px3] = cv
                    csp = transpose_128(xn_bf, 32,
                                        lambda j0, jn, pt=pt: xnf2[pt][:, j0:j0 + jn, :],
                                        [(VD, cv)], evac="alt", px2=px_big3)
                    xn_mark[0] = p.v(TD)
                    for i in range(32):
                        wt = []
                        if i == 0:
                            wt = [(SD, csp[0]), (VD, max(csp[1], p.v(VD)))]
                            if t == 0:
                                wt += [lg_w, la1]
                        if i == 16 and t == 0:
                            wt = [la2]
                        p.do("tensor", lambda e, i=i, pt=pt: e.matmul(
                            hps2[pt][:, 0:512], xnf2[pt][:, i, :], a_v[:, i, 0:512],
                            start=(i == 0), stop=(i == 31)), waits=wt, inc=TD)
                        p.do("tensor", lambda e, i=i, pt=pt: e.matmul(
                            hps2[pt][:, 512:1024], xnf2[pt][:, i, :], a_v[:, i, 512:1024],
                            start=(i == 0), stop=(i == 31)), inc=TD)
                        p.do("tensor", lambda e, i=i, pt=pt: e.matmul(
                            rps2[pt], xnf2[pt][:, i, :], wr_sb[:, i, :],
                            start=(i == 0), stop=(i == 31)), inc=TD)
                    mm_mark[t] = p.v(TD)
                if t >= 1:
                    tp = t - 1
                    cv = ffn_gate(tp, hps2[tp % 2], rps2[tp % 2])
                    hpb = hp_bf if tp % 2 == 0 else hp_bf2
                    transpose_128(hpb, 8,
                                  lambda j0, jn, tp=tp: hpf_v[:, j0:j0 + jn, 128 * tp:128 * (tp + 1)],
                                  [(VD, cv)], evac="scalar", px2=px_big3)
                    hp_transp_mark[tp % 2] = p.v(TD)

        def ffn_b(x_src, dst, do_ln):
            lg_w = dma("gpsimd", None, bt_v, b_t.rearrange("(j p) d -> p j d", p=128),
                       waits=[(TD, p.v(TD)), (VD, p.v(VD)), (SD, p.v(SD))])
            regs = [ps_big[:, 0:1024], ps_big[:, 1024:2048]]
            xb3 = [xbuf2[0], xbuf2[1], xbuf3]
            stt_mark = [0, 0]
            free_mark = [0, 0, 0]
            store_mark = [[], [], []]
            for t in range(NT + 1):
                if t < NT:
                    pt = t % 3
                    lx = dma("sync", None, xb3[pt], x_src[128 * t:128 * (t + 1), :],
                             waits=[(VD, free_mark[pt])] + store_mark[pt])
                    for q in range(4):
                        pq = q % 2
                        for j in range(8):
                            wt = []
                            if j == 0:
                                wt = [(VD, stt_mark[pq])]
                                if t == 0 and q == 0:
                                    wt.append(lg_w)
                            for f in range(2):
                                p.do("tensor", lambda e, j=j, f=f, t=t, q=q, pq=pq: e.matmul(
                                    regs[pq][:, 512 * f:512 * (f + 1)],
                                    hpf_v[:, j, 128 * t:128 * (t + 1)],
                                    bt_v[:, j, 1024 * q + 512 * f:1024 * q + 512 * (f + 1)],
                                    start=(j == 0), stop=(j == 7)),
                                    waits=(wt if f == 0 else []), inc=TD)
                        ct = p.v(TD)
                        wl = [(TD, ct), lx]
                        for f in range(2):
                            p.do("vector", lambda e, f=f, q=q, t=t, pq=pq: e.scalar_tensor_tensor(
                                xb3[t % 3][:, 1024 * q + 512 * f:1024 * q + 512 * (f + 1)],
                                regs[pq][:, 512 * f:512 * (f + 1)],
                                rz_sb[:, t:t + 1],
                                xb3[t % 3][:, 1024 * q + 512 * f:1024 * q + 512 * (f + 1)],
                                AluOpType.mult, AluOpType.add),
                                waits=(wl if f == 0 else []), inc=VD)
                        stt_mark[pq] = p.v(VD)
                    cv = p.v(VD)
                    store_mark[pt] = [dma("sync", None, dst[128 * t:128 * (t + 1), :],
                                          xb3[pt], waits=[(VD, cv)])]
                if do_ln and t == 5:
                    cc_marks["ag1a"] = p.do("gpsimd", lambda e: e.collective_compute(
                        "AllGather", mybir.AluOpType.bypass, replica_groups=groups,
                        ins=[agin[0].opt()], outs=[agout[0].opt()]),
                        waits=ag_half_mark[0] + [(CC, p.v(CC))], inc=CC)
                if t >= 1 and do_ln:
                    tp = t - 1
                    cv = ln_tile(xb3[tp % 3], xn_bf)
                    free_mark[tp % 3] = cv
                    csp = transpose_128(xn_bf, 32,
                                         lambda j0, jn, tp=tp: xnf2[tp % 2][:, j0:j0 + jn, :],
                                         [(VD, cv)], copy_waits=fence("sync"),
                                         evac="alt", px2=px_med)
                    xn_mark[0] = p.v(TD)
                    dma("sync", None,
                        agin[tp // 4][0, :, :, 128 * (tp % 4):128 * (tp % 4 + 1)].rearrange("c p t -> p c t"),
                        xnf2[tp % 2], waits=[(SD, csp[0]), (VD, csp[1])])
                    if tp == 3:
                        ag_half_mark[0] = fence("sync")
                elif t >= 1:
                    free_mark[(t - 1) % 3] = p.v(VD)

        ffn_a(x_in)
        p.barrier()
        ffn_b(x_in, sc_x1, True)
        p.barrier()

        cc1 = p.do("gpsimd", lambda e: e.collective_compute(
            "AllGather", mybir.AluOpType.bypass, replica_groups=groups,
            ins=[agin[1].opt()], outs=[agout[1].opt()]),
            waits=fence("sync") + [(CC, p.v(CC))], inc=CC)

        xl_tok = {0: [], 1: []}
        for r in range(2):
            for lo in range(2):
                view = xn1a_v if lo == 0 else xn1b_v
                for u in range(2):
                    ccw = cc_marks["ag1a"] if u == 0 else cc1
                    xl_tok[u].append(dma(
                        "sync", None,
                        view[:, :, 1024 * r + 512 * u:1024 * r + 512 * (u + 1)],
                        agout[u][r, 16 * lo:16 * (lo + 1), :, :].rearrange("c p t -> p c t"),
                        waits=[(CC, ccw)]))
        tg_order = [0, 2, 1, 3]
        tg_w = {0: xl_tok[0], 2: xl_tok[0], 1: xl_tok[1], 3: xl_tok[1]}

        projs = [(sc_q, bqs_sb, QSCALE), (sc_k, bk_sb, 1.0), (sc_v, bv_sb, 1.0)]
        slot_mark = [0, 0]
        qe_mark = [[], []]
        v_mark = []
        for pj in range(3):
            for h in range(HL):
                k_ = pj * HL + h
                slot = k_ % 2
                wb = w_buf[slot]
                qe = qe_buf[slot]
                lw = dma("gpsimd", None, wb,
                         wqkv[pj, h].rearrange("(i p) f -> p i f", p=128),
                         waits=[(TD, slot_mark[slot])])
                for i in range(32):
                    lo, iv = (0, i) if i < 16 else (1, i - 16)
                    view = xn1a_v if lo == 0 else xn1b_v
                    for tg in tg_order:
                        wt = []
                        if i == 0:
                            wt = [lw, (SD, p.v(SD))] + tg_w[tg]
                        p.do("tensor", lambda e, i=i, tg=tg, view=view, iv=iv, wb=wb: e.matmul(
                            ps_big[:, 512 * tg:512 * (tg + 1)],
                            wb[:, i, :], view[:, iv, 512 * tg:512 * (tg + 1)],
                            start=(i == 0), stop=(i == 31)),
                            waits=wt, inc=TD)
                slot_mark[slot] = p.v(TD)
                ct = p.v(TD)
                dst, bias, scale = projs[pj]
                cs = p.do("scalar", lambda e, bias=bias, scale=scale, h=h, qe=qe: e.activation(
                    qe, ps_big[:], AF.Identity, bias=bias[:, h:h + 1], scale=scale),
                    waits=[(TD, ct)] + qe_mark[slot], inc=SD)
                if pj < 2:
                    qe_mark[slot] = [dma("sync", None, dst[h], qe, waits=[(SD, cs)])]
                else:
                    csp2 = transpose_128(qe, 16, lambda j0, jn: v_buf[:, j0:j0 + jn, :],
                                          [(SD, cs)], copy_waits=v_mark,
                                          evac="alt", px2=px_med)
                    v_mark = [dma("sync", None,
                                  sc_v[:, :, h, :].rearrange("c p d -> p c d"),
                                  v_buf, waits=[(SD, csp2[0]), (VD, csp2[1])])]
                    qe_mark[slot] = v_mark
        p.barrier()

        exp_mark = [0, 0, 0, 0]
        negm_mark = [0, 0]
        ld_mark = [[], []]
        cevac_mark = 0
        cstore_mark = [[], []]

        def p4_loads(h):
            par = h % 2
            wt = [(TD, p.v(TD)), (SD, p.v(SD)), (VD, p.v(VD))]
            t1_ = dma("sync", None, k_sb2[par][:], sc_k[h], waits=wt)
            t2_ = dma("sync", None, q_sb2[par][:], sc_q[h], waits=wt)
            t3_ = dma("sync", None, v_ts2[par],
                      sc_v[:, :, h, :].rearrange("c p d -> p c d"), waits=wt)
            ld_mark[par] = [t1_, t2_, t3_]

        def p4_alk(h):
            lk = dma("gpsimd", None, alk_sb[:], alk3[:, h, :], waits=[(TD, p.v(TD))])
            return lk

        def p4_scores(h, t, lk_a):
            par = h % 2
            nk = 128 * (t + 1)
            nkb = (nk + 511) // 512
            j0w = max(0, 128 * t - WINS[h]) // 512
            blk_ct = {}
            for j in range(j0w, nkb):
                w = min(512, nk - 512 * j)
                wt = [(SD, exp_mark[j])]
                if j == j0w:
                    wt += ld_mark[par] + [lk_a]
                p.do("tensor", lambda e, j=j, w=w, t=t, par=par: e.matmul(
                    ps_big[:, 512 * j:512 * j + w],
                    q_sb2[par][:, 128 * t:128 * (t + 1)],
                    k_sb2[par][:, 512 * j:512 * j + w],
                    start=True, stop=False), waits=wt, inc=TD)
                p.do("tensor", lambda e, j=j, w=w: e.matmul(
                    ps_big[:, 512 * j:512 * j + w],
                    ones3_sb[:], alk_sb[:, 512 * j:512 * j + w],
                    start=False, stop=True), inc=TD)
                blk_ct[j] = p.v(TD)
            dj = t // 4
            for j in range(j0w, nkb):
                w = min(512, nk - 512 * j)
                if j == dj:
                    p.do("vector", lambda e, t=t: e.tensor_tensor(
                        ps_big[:, 128 * t:128 * (t + 1)],
                        ps_big[:, 128 * t:128 * (t + 1)],
                        tri_sb[:], AluOpType.add), waits=[(TD, blk_ct[j])], inc=VD)
                    p.do("vector", lambda e: e.drain())
                p.do("vector", lambda e, j=j, w=w: e.reduce_max(
                    bm_sb[:, j:j + 1], ps_big[:, 512 * j:512 * j + w], axis=AX),
                    waits=([(TD, blk_ct[j])] if j != dj else []), inc=VD)
            return (j0w, nkb)

        def p4_combine(h, t, jb):
            j0w, nkb = jb
            tp = t % 2
            p.do("vector", lambda e: e.drain(), waits=[(SD, negm_mark[tp])])
            cv = p.do("vector", lambda e, j0w=j0w, nkb=nkb, tp=tp: e.reduce_max(
                negm2[:, tp:tp + 1], bm_sb[:, j0w:nkb], axis=AX, negate=True), inc=VD)
            return cv

        def p4_exp(h, t, jb, cv):
            j0w, nkb = jb
            nk = 128 * (t + 1)
            tp = t % 2
            e_sb = e_sb2[tp]
            for j in range(j0w, nkb):
                w = min(512, nk - 512 * j)
                exp_mark[j] = p.do("scalar", lambda e, j=j, w=w, e_sb=e_sb, tp=tp: e.activation(
                    e_sb[:, 512 * j:512 * j + w], ps_big[:, 512 * j:512 * j + w],
                    AF.Exp, bias=negm2[:, tp:tp + 1], scale=1.0, accum_out=zb4[:, j:j + 1]),
                    waits=([(VD, cv)] if j == j0w else []), inc=SD)
            negm_mark[tp] = p.v(SD)

        def p4_norm(h, t, jb):
            j0w, nkb = jb
            nk = 128 * (t + 1)
            tp = t % 2
            e_sb = e_sb2[tp]
            p.do("vector", lambda e, j0w=j0w, nkb=nkb: e.reduce_sum(
                zacc[:], zb4[:, j0w:nkb], axis=AX),
                waits=[(SD, exp_mark[nkb - 1])], inc=VD)
            p.do("vector", lambda e: e.drain())
            p.do("vector", lambda e: e.reciprocal(rz1[:], zacc[:]), inc=VD)
            p.do("vector", lambda e: e.drain())
            cv = p.do("vector", lambda e, j0w=j0w, nk=nk, e_sb=e_sb: e.tensor_scalar(
                e_sb[:, 512 * j0w:nk], e_sb[:, 512 * j0w:nk], rz1[:], None,
                AluOpType.mult), inc=VD)
            return cv

        def p4_transp(h, t, cv_norm):
            tp = t % 2
            j0w = max(0, 128 * t - WINS[h]) // 512
            c0 = 4 * j0w
            return transpose_128(e_sb2[tp][:, 128 * c0:], t + 1 - c0,
                                 lambda j0, jn, et_sb=et_sb2[tp], c0=c0: et_sb[:, 128 * (c0 + j0):128 * (c0 + j0 + jn)],
                                 [(VD, cv_norm)], evac="alt", px2=px_med)

        def p4_pv(h, t, cse):
            nonlocal cevac_mark
            par = h % 2
            tp = t % 2
            et_sb = et_sb2[tp]
            c0 = 4 * (max(0, 128 * t - WINS[h]) // 512)
            for j2 in range(c0, t + 1):
                wt = []
                if j2 == c0:
                    wt = [(SD, max(cse[0], cevac_mark)), (VD, cse[1])]
                p.do("tensor", lambda e, j2=j2, t=t, par=par, et_sb=et_sb, c0=c0: e.matmul(
                    ps_r[:, 0:128], v_ts2[par][:, j2, :],
                    et_sb[:, 128 * j2:128 * (j2 + 1)],
                    start=(j2 == c0), stop=(j2 == t)), waits=wt, inc=TD)
            ct = p.v(TD)
            cevac_mark = p.do("scalar", lambda e, t=t, par=par: e.copy(
                ctxh_sb2[par][:, 128 * t:128 * (t + 1)], ps_r[:, 0:128]),
                waits=[(TD, ct)] + cstore_mark[par], inc=SD)

        p4_loads(0)
        lk_a = p4_alk(0)
        prev = None
        for h in range(HL):
            for t in range(16):
                jb = p4_scores(h, t, lk_a)
                cvc = p4_combine(h, t, jb)
                if prev is not None:
                    cse = p4_transp(*prev)
                p4_exp(h, t, jb, cvc)
                if prev is not None:
                    p4_pv(prev[0], prev[1], cse)
                cvn = p4_norm(h, t, jb)
                prev = (h, t, cvn)
                if t == 1 and h + 1 < HL:
                    p4_loads(h + 1)
            if h + 1 < HL:
                lk_a = p4_alk(h + 1)
            if h > 0:
                cstore_mark[(h - 1) % 2] = [dma(
                    "sync", None, cin[(h - 1) // 8][0, (h - 1) % 8],
                    ctxh_sb2[(h - 1) % 2][:], waits=[(SD, cevac_mark)])]
            if h == 9:
                cc_marks["ag2a"] = p.do("gpsimd", lambda e: e.collective_compute(
                    "AllGather", mybir.AluOpType.bypass, replica_groups=groups,
                    ins=[cin[0].opt()], outs=[cout[0].opt()]),
                    waits=fence("sync") + [(CC, p.v(CC))], inc=CC)
        cse = p4_transp(*prev)
        p4_pv(prev[0], prev[1], cse)
        cstore_mark[(HL - 1) % 2] = [dma(
            "sync", None, cin[1][0, 7], ctxh_sb2[(HL - 1) % 2][:],
            waits=[(SD, cevac_mark)])]
        p.barrier()

        cc2 = p.do("gpsimd", lambda e: e.collective_compute(
            "AllGather", mybir.AluOpType.bypass, replica_groups=groups,
            ins=[cin[1].opt()], outs=[cout[1].opt()]),
            waits=fence("sync") + [(CC, p.v(CC))], inc=CC)

        mg_done = {}
        for g in [0, 2, 1, 3]:
            r, u = g // 2, g % 2
            src = cout[u][r].rearrange("c p t -> p c t")
            wA = [(CC, cc_marks["ag2a"] if u == 0 else cc2), (VD, p.v(VD))]
            if g == 0:
                wA += [(TD, p.v(TD)), (SD, p.v(SD))]
            lA = dma("sync", None, mgA, src[:, :, 0:1024], waits=wA)
            lB = dma("sync", None, mgB, src[:, :, 1024:2048], waits=wA)
            p.do("vector", lambda e: e.tensor_tensor(
                mgB, mgB, mgA, AluOpType.subtract), waits=[lA, lB], inc=VD)
            p.do("vector", lambda e: e.drain())
            mg_done[g] = p.do("vector", lambda e, g=g: e.scalar_tensor_tensor(
                ctx_v[:, 8 * g:8 * (g + 1), :], mgB, msel_sb[:], mgA,
                AluOpType.mult, AluOpType.add), inc=VD)
            p.do("vector", lambda e: e.drain())
        iorder = list(range(0, 8)) + list(range(16, 24)) + list(range(8, 16)) + list(range(24, 32))
        cv_a = max(mg_done[0], mg_done[2])
        cv_ctx = p.v(VD)

        slot_mark2 = [p.v(TD), p.v(TD)]
        for fc in range(32):
            slot = fc % 2
            wb = w_buf[slot]
            wdma = [(TD, slot_mark2[slot])]
            if fc < 2:
                wdma.append((VD, cv_ctx))
            lw = dma("gpsimd", None, wb, wout[fc].rearrange("(i p) f -> p i f", p=128),
                     waits=wdma)
            for n_, i in enumerate(iorder):
                wt = []
                if n_ == 0:
                    wt = [lw, (VD, cv_a), (SD, p.v(SD))]
                elif n_ == 16:
                    wt = [(VD, cv_ctx)]
                for tg in range(2):
                    p.do("tensor", lambda e, i=i, tg=tg, wb=wb, n_=n_: e.matmul(
                        ps_med[:, 512 * tg:512 * (tg + 1)],
                        wb[:, i, :], ctx_v[:, i, 512 * tg:512 * (tg + 1)],
                        start=(n_ == 0), stop=(n_ == 31)),
                        waits=(wt if tg == 0 else []), inc=TD)
            slot_mark2[slot] = p.v(TD)
            ct = p.v(TD)
            cs = p.do("scalar", lambda e, fc=fc: e.activation(
                aoe_buf, ps_med[:], AF.Identity, bias=bo_sb[:, fc:fc + 1], scale=1.0),
                waits=[(TD, ct)], inc=SD)
            transpose_128(aoe_buf, 8,
                          lambda j0, jn, fc=fc: ao_v[:, j0:j0 + jn, 128 * fc:128 * (fc + 1)],
                          [(SD, cs)], evac="alt", px2=px_big3)
        p.barrier()
        for t in range(NT):
            lx = dma("sync", None, xt_p6, sc_x1[128 * t:128 * (t + 1), :],
                     waits=[(VD, p.v(VD))] + fence("sync"))
            cv = p.do("vector", lambda e, t=t: e.tensor_tensor(
                xt_p6, xt_p6, ao_v[:, t, :], AluOpType.add),
                waits=[lx], inc=VD)
            p.do("vector", lambda e: e.drain())
            dma("sync", None, sc_x2[128 * t:128 * (t + 1), :], xt_p6, waits=[(VD, cv)])
        p.barrier()

        ffn_a(sc_x2)
        p.barrier()
        ffn_b(sc_x2, out_ext, False)
        p.barrier()

        block = ctx.enter_context(nc.Block())

        def mk(name):
            def run(eng):
                for waits, fn, inc, n in p.ops[name]:
                    for s_, v_ in waits:
                        if v_ > 0:
                            eng.wait_ge(sems[s_], v_)
                    if fn is not None:
                        inst = fn(eng)
                        if inc is not None:
                            inst.then_inc(sems[inc], n)
            return run

        block.tensor(mk("tensor"))
        block.vector(mk("vector"))
        block.scalar(mk("scalar"))
        block.sync(mk("sync"))
        block.gpsimd(mk("gpsimd"))

    return nc


_NC_CACHE = {}


def _get_nc(dbg=False):
    if dbg not in _NC_CACHE:
        _NC_CACHE[dbg] = build(dbg)
    return _NC_CACHE[dbg]


def make_in_maps(L0_attn_hidden_states, L0_attn_w_qkv, L0_attn_b_qkv,
                 L0_attn_w_out, L0_attn_b_out, L0_ffn_A, L0_ffn_B, L0_ffn_Wr):
    import ml_dtypes
    x = np.ascontiguousarray(np.asarray(L0_attn_hidden_states, dtype=np.float32))
    w_qkv = np.asarray(L0_attn_w_qkv, dtype=np.float32)
    b_qkv = np.asarray(L0_attn_b_qkv, dtype=np.float32)
    w_out = np.asarray(L0_attn_w_out, dtype=np.float32)
    b_out = np.asarray(L0_attn_b_out, dtype=np.float32)
    A = np.asarray(L0_ffn_A, dtype=np.float32)
    Bm = np.asarray(L0_ffn_B, dtype=np.float32)
    Wr = np.asarray(L0_ffn_Wr, dtype=np.float32)

    slopes = get_alibi_slopes(32)
    kpos = np.arange(S, dtype=np.float32)

    a_fm = np.ascontiguousarray(A.reshape(KR, D).T)
    b_t = np.ascontiguousarray(Bm.transpose(0, 2, 1).reshape(KR, D))
    wr_fm = np.ascontiguousarray(Wr.T)
    ident = np.eye(128, dtype=np.float32)
    tri = np.where(np.arange(128)[None, :] > np.arange(128)[:, None],
                   np.float32(NEG), np.float32(0.0)).astype(np.float32)
    ones3 = np.ones((3, 128), dtype=np.float32)
    bo_col = np.ascontiguousarray(b_out.reshape(32, 128).T)

    def bf(v):
        return v.astype(ml_dtypes.bfloat16).astype(np.float32)

    perm = ([2 * i for i in range(8)] + [2 * i for i in range(8, 16)]
            + [2 * i + 1 for i in range(8)] + [2 * i + 1 for i in range(8, 16)])
    wo_p = np.ascontiguousarray(
        w_out.reshape(32, 128, D)[perm].reshape(D, D))
    wo = np.ascontiguousarray(wo_p.reshape(D, 32, 128).transpose(1, 0, 2))
    in_maps = []
    for c in range(8):
        b = c // 2
        half = c % 2
        gheads = [2 * i + half for i in range(HL)]
        wq = np.empty((3, HL, D, HD), dtype=np.float32)
        for pj in range(3):
            for h in range(HL):
                col0 = pj * D + gheads[h] * HD
                wq[pj, h] = w_qkv[:, col0:col0 + HD]
        bq_col = np.empty((128, HL), dtype=np.float32)
        bk_col = np.empty((128, HL), dtype=np.float32)
        bv_col = np.empty((128, HL), dtype=np.float32)
        for h in range(HL):
            g = gheads[h]
            bq_col[:, h] = b_qkv[0 * D + g * HD: 0 * D + g * HD + HD]
            bk_col[:, h] = b_qkv[1 * D + g * HD: 1 * D + g * HD + HD]
            bv_col[:, h] = b_qkv[2 * D + g * HD: 2 * D + g * HD + HD]
        alk3_a = np.empty((3, HL, S), dtype=np.float32)
        for h in range(HL):
            v = slopes[gheads[h]] * kpos
            hi = bf(v)
            mid = bf(v - hi)
            lo = bf(v - hi - mid)
            alk3_a[0, h], alk3_a[1, h], alk3_a[2, h] = hi, mid, lo
        in_maps.append({
            "x": np.ascontiguousarray(x[b, half * TOWN:(half + 1) * TOWN, :]),
            "wqkv": wq,
            "wout": wo,
            "a_fm": a_fm,
            "b_t": b_t,
            "wr_fm": wr_fm,
            "bq_col": bq_col,
            "bk_col": bk_col,
            "bv_col": bv_col,
            "bo_col": bo_col,
            "ident": ident,
            "tri": tri,
            "alk3": alk3_a,
            "ones3": ones3,
            "msel": np.full((128, 1), float(half), dtype=np.float32),
        })
    return in_maps


def kernel(L0_attn_hidden_states, L0_attn_w_qkv, L0_attn_b_qkv, L0_attn_w_out,
           L0_attn_b_out, L0_ffn_A, L0_ffn_B, L0_ffn_Wr):
    in_maps = make_in_maps(L0_attn_hidden_states, L0_attn_w_qkv, L0_attn_b_qkv,
                           L0_attn_w_out, L0_attn_b_out, L0_ffn_A, L0_ffn_B,
                           L0_ffn_Wr)
    nc = _get_nc(False)
    res = run_bass_kernel_spmd(nc, in_maps, core_ids=list(range(8)))
    out = np.empty((B, S, D), dtype=np.float32)
    for c in range(8):
        b = c // 2
        half = c % 2
        out[b, half * TOWN:(half + 1) * TOWN, :] = res.results[c]["out"]
    return out


# version 21
# speedup vs baseline: 1.0564x; 1.0564x over previous
import math
from contextlib import ExitStack

import numpy as np
import concourse.bass as bass
import concourse.mybir as mybir
from concourse.alu_op_type import AluOpType
from concourse.bass_utils import run_bass_kernel_spmd

F32 = mybir.dt.float32
BF16 = mybir.dt.bfloat16
AF = mybir.ActivationFunctionType
AX = mybir.AxisListType.X

B, S, D = 4, 2048, 4096
HL, HD = 16, 128
TOWN = 1024
NT = TOWN // 128
KEXP, RANK = 64, 16
KR = KEXP * RANK
QSCALE = HD ** (-0.5)
EPS = 1e-5
NEG = -1.0e9

ENGS = ("tensor", "vector", "scalar", "sync", "gpsimd")
CC, TD, VD, SD = "cc", "td", "vd", "sd"
SYNC_SEMS = ("ly0", "ly1", "ly2", "ly3")
GP_SEMS = ("lg0", "lg1")
ALL_SEMS = SYNC_SEMS + GP_SEMS + (CC, TD, VD, SD)


def get_alibi_slopes(n):
    cp2 = 2 ** math.floor(math.log2(n))
    base = 2 ** (-(2 ** (-(math.log2(cp2) - 3))))
    powers = np.arange(1, 1 + cp2, dtype=np.float32)
    slopes = np.power(base, powers).astype(np.float32)
    if cp2 != n:
        eb = 2 ** (-(2 ** (-(math.log2(2 * cp2) - 3))))
        ep = np.arange(1, 1 + 2 * min(cp2, n - cp2), 2, dtype=np.float32)
        slopes = np.concatenate([slopes, np.power(eb, ep).astype(np.float32)])
    return slopes


class Plan:
    def __init__(self):
        self.ops = {k: [] for k in ENGS}
        self.cnt = {}

    def do(self, eng, fn, waits=(), inc=None, n=1):
        if inc is not None:
            self.cnt[inc] = self.cnt.get(inc, 0) + n
        self.ops[eng].append((tuple(waits), fn, inc, n))
        return self.cnt.get(inc, 0) if inc is not None else None

    def v(self, sem):
        return self.cnt.get(sem, 0)

    def barrier(self):
        snap = tuple((k, v) for k, v in self.cnt.items() if v > 0)
        for eng in ENGS:
            self.ops[eng].append((snap, None, None, 0))


WIN_MARGIN = 164.0


def local_windows():
    slopes = get_alibi_slopes(32)
    wins = []
    for i in range(HL):
        sl = min(slopes[2 * i], slopes[2 * i + 1])
        w = int(math.ceil(WIN_MARGIN / sl))
        wins.append(min(w, S))
    return wins


def build(dbg=False):
    nc = bass.Bass("TRN2", target_bir_lowering=False)
    WINS = local_windows()

    def inp(name, shape):
        return nc.dram_tensor(name, shape, F32, kind="ExternalInput").ap()

    x_in = inp("x", [TOWN, D])
    wqkv = inp("wqkv", [3, HL, D, HD])
    wout = inp("wout", [32, D, 128])
    a_fm = inp("a_fm", [D, KR])
    b_t = inp("b_t", [KR, D])
    wr_fm = inp("wr_fm", [D, KEXP])
    bq_col = inp("bq_col", [128, HL])
    bk_col = inp("bk_col", [128, HL])
    bv_col = inp("bv_col", [128, HL])
    bo_col = inp("bo_col", [128, 32])
    ident = inp("ident", [128, 128])
    tri = inp("tri", [128, 128])
    alk3 = inp("alk3", [3, HL, S])
    ones3 = inp("ones3", [3, 128])
    msel = inp("msel", [128, 1])

    out_ext = nc.dram_tensor("out", [TOWN, D], F32, kind="ExternalOutput").ap()

    def scratch(name, shape, dt):
        kind = "ExternalOutput" if dbg else "Internal"
        return nc.dram_tensor(name, shape, dt, kind=kind).ap()

    sc_x1 = scratch("sc_x1", [TOWN, D], F32)
    sc_x2 = scratch("sc_x2", [TOWN, D], F32)
    sc_q = scratch("sc_q", [HL, 128, S], BF16)
    sc_k = scratch("sc_k", [HL, 128, S], BF16)
    sc_v = scratch("sc_v", [16, 128, HL, 128], BF16)
    agin = [nc.dram_tensor("agin0", [1, 32, 128, 512], BF16).ap(),
            nc.dram_tensor("agin1", [1, 32, 128, 512], BF16).ap()]
    agout = [nc.dram_tensor("agout0", [2, 32, 128, 512], BF16).ap(),
             nc.dram_tensor("agout1", [2, 32, 128, 512], BF16).ap()]
    cin = [nc.dram_tensor("cin0", [1, 8, 128, S], BF16).ap(),
           nc.dram_tensor("cin1", [1, 8, 128, S], BF16).ap()]
    cout = [nc.dram_tensor("cout0", [2, 8, 128, S], BF16).ap(),
            nc.dram_tensor("cout1", [2, 8, 128, S], BF16).ap()]
    groups = [[0, 1], [2, 3], [4, 5], [6, 7]]

    p = Plan()

    with ExitStack() as ctx:
        def sb(name, shape, dt):
            return ctx.enter_context(nc.sbuf_tensor(name, shape, dt))

        sems = {}
        for s_ in ALL_SEMS:
            sems[s_] = ctx.enter_context(nc.semaphore(s_))

        ident_sb = sb("ident_sb", [128, 128], BF16)
        tri_sb = sb("tri_sb", [128, 128], F32)
        ones3_sb = sb("ones3_sb", [3, 128], BF16)
        alk_sb = sb("alk_sb", [3, S], BF16)
        bq_sb = sb("bq_sb", [128, HL], F32)
        bk_sb = sb("bk_sb", [128, HL], F32)
        bv_sb = sb("bv_sb", [128, HL], F32)
        bo_sb = sb("bo_sb", [128, 32], F32)
        bqs_sb = sb("bqs_sb", [128, HL], F32)
        msel_sb = sb("msel_sb", [128, 1], F32)

        bnst = sb("bnst", [128, 48], F32)
        mv = sb("mv", [128, 2], F32)
        t1 = sb("t1", [128, 1], F32)
        rstd = sb("rstd", [128, 1], F32)
        negm = sb("negm", [128, 1], F32)
        negm2 = sb("negm2", [128, 2], F32)
        zacc = sb("zacc", [128, 1], F32)
        rz1 = sb("rz1", [128, 1], F32)
        scr = sb("scr", [128, KEXP], F32)
        top8a = sb("top8a", [128, 8], F32)
        top8b = sb("top8b", [128, 8], F32)
        rep = sb("rep", [128, KEXP], F32)
        e_r = sb("e_r", [128, KEXP], F32)
        selm = sb("selm", [128, KEXP], F32)
        eg = sb("eg", [128, KEXP], F32)
        zg = sb("zg", [128, 1], F32)
        rz_sb = sb("rz_sb", [128, NT], F32)

        ar1 = sb("ar1", [128, 32768], BF16)
        ar2 = sb("ar2", [128, 32768], BF16)
        pool = sb("pool", [128, 16384], BF16)
        wr_sb = sb("wr_sb", [128, 32, KEXP], BF16)
        xn_bf = sb("xn_bf", [128, D], BF16)
        hp_bf = sb("hp_bf", [128, KR], BF16)
        hp_bf2 = sb("hp_bf2", [128, KR], BF16)

        a_v = ar1[:].rearrange("p (i k) -> p i k", i=32)
        bt_v = ar1[:].rearrange("p (j d) -> p j d", j=8)
        ao_v = ar1[:].rearrange("p (t f) -> p t f", t=8)
        xn1a_v = ar1[:].rearrange("p (i t) -> p i t", i=16)
        xn1b_v = ar2[:].rearrange("p (i t) -> p i t", i=16)
        ctx_v = ar2[:].rearrange("p (i t) -> p i t", i=32)

        ar2f = ar2[:].bitcast(F32)
        hpf_v = ar2[:, 0:8192].rearrange("p (j t) -> p j t", j=8)
        xnf2 = [ar2[:, 8192:12288].rearrange("p (i t) -> p i t", i=32),
                ar2[:, 12288:16384].rearrange("p (i t) -> p i t", i=32)]
        xbuf2 = [ar2f[:, 8192:12288], ar2f[:, 12288:16384]]

        poolf = pool[:].bitcast(F32)
        xbuf3 = poolf[:, 0:4096]
        w_buf = [pool[:, 0:4096].rearrange("p (i f) -> p i f", i=32),
                 pool[:, 4096:8192].rearrange("p (i f) -> p i f", i=32)]
        qe_buf = [pool[:, 8192:10240], pool[:, 10240:12288]]
        v_buf = pool[:, 12288:14336].rearrange("p (c f) -> p c f", c=16)
        k_sb2 = [ar2[:, 0:2048], ar2[:, 2048:4096]]
        q_sb2 = [ar2[:, 4096:6144], ar2[:, 6144:8192]]
        v_ts2 = [ar2[:, 8192:10240].rearrange("p (c f) -> p c f", c=16),
                 ar2[:, 10240:12288].rearrange("p (c f) -> p c f", c=16)]
        ctxh_sb2 = [ar2[:, 12288:14336], ar2[:, 14336:16384]]
        e_sb2 = [ar2[:, 16384:18432], ar2[:, 18432:20480]]
        et_sb2 = [ar2[:, 20480:22528], ar2[:, 22528:24576]]
        bm_sb = sb("bm_sb", [128, 4], F32)
        zb4 = sb("zb4", [128, 4], F32)
        mgA = pool[:, 0:8192].rearrange("p (c t) -> p c t", c=8)
        mgB = pool[:, 8192:16384].rearrange("p (c t) -> p c t", c=8)
        aoe_buf = pool[:, 8192:9216]
        xt_p6 = poolf[:, 0:4096]

        ps_big = ctx.enter_context(nc.psum_tensor("ps_big", [128, 2048], F32))
        ps_med = ctx.enter_context(nc.psum_tensor("ps_med", [128, 1024], F32))
        ps_r = ctx.enter_context(nc.psum_tensor("ps_r", [128, 512], F32))
        ps_x = ctx.enter_context(nc.psum_tensor("ps_x", [128, 1024], BF16))
        px_med = ps_med[:, 0:512].bitcast(BF16)
        px_big3 = ps_big[:, 1536:2048].bitcast(BF16)

        mm_mark = {}
        hp_transp_mark = [0, 0]
        xn_mark = [0]
        ag_half_mark = [[]]
        cc_marks = {}

        rot = {"sync": list(SYNC_SEMS), "gpsimd": list(GP_SEMS)}
        rix = {"sync": 0, "gpsimd": 0}

        def dma(eng, _sem, out, in_, waits=()):
            i = rix[eng]
            rix[eng] = (i + 1) % len(rot[eng])
            s_ = rot[eng][i]
            wt = [(s_, p.v(s_))] + list(waits)
            c = p.do(eng, lambda e, o=out, i_=in_: e.dma_start(out=o, in_=i_),
                     waits=wt, inc=s_, n=16)
            return (s_, c)

        def fence(eng):
            return [(s_, p.v(s_)) for s_ in rot[eng]]

        def fence_all():
            return fence("sync") + fence("gpsimd")

        dma("gpsimd", None, ident_sb[:], ident)
        dma("gpsimd", None, ones3_sb[:], ones3)
        dma("sync", None, tri_sb[:], tri)
        dma("sync", None, bq_sb[:], bq_col)
        dma("sync", None, bk_sb[:], bk_col)
        dma("sync", None, bv_sb[:], bv_col)
        dma("sync", None, bo_sb[:], bo_col)
        dma("sync", None, msel_sb[:], msel)
        p.do("vector", lambda e: e.tensor_scalar(
            bqs_sb[:], bq_sb[:], QSCALE, None, AluOpType.mult),
            waits=fence("sync"), inc=VD)
        p.barrier()

        def ln_tile(x_ap, out_bf):
            p.do("vector", lambda e: e.drain())
            for i in range(8):
                p.do("vector", lambda e, i=i: e.bn_stats(
                    bnst[:, 6 * i:6 * i + 6], x_ap[:, 512 * i:512 * (i + 1)]), inc=VD)
            p.do("vector", lambda e: e.drain())
            p.do("vector", lambda e: e.bn_aggr(mv[:], bnst[:]), inc=VD)
            p.do("vector", lambda e: e.drain())
            cv = p.do("vector", lambda e: e.tensor_scalar(
                t1[:], mv[:, 1:2], EPS, None, AluOpType.add), inc=VD)
            cs = p.do("scalar", lambda e: e.activation(rstd[:], t1[:], AF.Sqrt),
                      waits=[(VD, cv)], inc=SD)
            p.do("vector", lambda e: e.reciprocal(rstd[:], rstd[:]),
                 waits=[(SD, cs)], inc=VD)
            p.do("vector", lambda e: e.drain())
            cv = p.do("vector", lambda e: e.tensor_scalar(
                out_bf[:], x_ap[:], mv[:, 0:1], rstd[:],
                AluOpType.subtract, AluOpType.mult),
                waits=[(TD, xn_mark[0])], inc=VD)
            return cv

        def transpose_128(src_bf, n128, dst_fn, waits0, copy_waits=(), evac="scalar",
                          px2=None):
            cs_pair = [p.v(SD), p.v(VD)]
            cw_done = set()
            PK = 8 if px2 is not None else 4
            rmark = {0: [p.v(SD), p.v(VD)], 1: [p.v(SD), p.v(VD)]}
            for j0 in range(0, n128, PK):
                jn = min(PK, n128 - j0)
                pk = (j0 // PK)
                reg = pk % 2 if px2 is not None else 0
                px = ps_x if reg == 0 else px2
                if evac == "alt":
                    ev = "scalar" if pk % 2 == 0 else "vector"
                else:
                    ev = evac
                for j in range(j0, j0 + jn):
                    wt = []
                    if j == j0:
                        wt = [(SD, rmark[reg][0]), (VD, rmark[reg][1])]
                    if j == 0:
                        wt = wt + list(waits0)
                    p.do("tensor", lambda e, j=j, px=px, PK=PK: e.transpose(
                        px[:, 128 * (j % PK):128 * (j % PK) + 128],
                        src_bf[:, 128 * j:128 * j + 128], ident_sb[:]),
                        waits=wt, inc=TD)
                ct = p.v(TD)
                dst = dst_fn(j0, jn)
                src = px[:, 0:128 * jn]
                if len(dst.shape) == 3:
                    src = src.rearrange("p (j f) -> p j f", j=jn)
                wt = [(TD, ct)]
                if ev not in cw_done:
                    wt += list(copy_waits)
                    cw_done.add(ev)
                if ev == "scalar":
                    cs = p.do("scalar", lambda e, dst=dst, src=src: e.copy(dst, src),
                              waits=wt, inc=SD)
                    rmark[reg][0] = cs
                    cs_pair[0] = cs
                else:
                    cs = p.do("vector", lambda e, dst=dst, src=src: e.tensor_copy(dst, src),
                              waits=wt, inc=VD)
                    rmark[reg][1] = cs
                    cs_pair[1] = cs
            return cs_pair

        def ffn_gate(t, hps, rps):
            csc = p.do("scalar", lambda e, rps=rps: e.copy(scr[:], rps),
                       waits=[(TD, mm_mark[t]), (VD, p.v(VD))], inc=SD)
            p.do("vector", lambda e: e.max(top8a[:], scr[:]),
                 waits=[(SD, csc)], inc=VD)
            p.do("vector", lambda e: e.drain())
            p.do("vector", lambda e: e.match_replace(rep[:], top8a[:], scr[:], -3.0e38), inc=VD)
            p.do("vector", lambda e: e.drain())
            p.do("vector", lambda e: e.max(top8b[:], rep[:]), inc=VD)
            p.do("vector", lambda e: e.drain())
            cv = p.do("vector", lambda e: e.tensor_scalar(
                negm[:], top8a[:, 0:1], -1.0, None, AluOpType.mult), inc=VD)
            p.do("vector", lambda e: e.drain())
            cs2 = p.do("scalar", lambda e: e.activation(
                e_r[:], scr[:], AF.Exp, bias=negm[:], scale=1.0),
                waits=[(VD, cv)], inc=SD)
            p.do("vector", lambda e: e.scalar_tensor_tensor(
                eg[:], scr[:], top8b[:, 7:8], e_r[:],
                AluOpType.is_ge, AluOpType.mult),
                waits=[(SD, cs2)], inc=VD)
            p.do("vector", lambda e: e.drain())
            p.do("vector", lambda e: e.reduce_sum(zg[:], eg[:], axis=AX), inc=VD)
            p.do("vector", lambda e: e.drain())
            p.do("vector", lambda e, t=t: e.reciprocal(rz_sb[:, t:t + 1], zg[:]), inc=VD)
            hpb = hp_bf if t % 2 == 0 else hp_bf2
            cv = p.do("vector", lambda e, hps=hps, hpb=hpb: e.tensor_tensor(
                hpb[:].rearrange("p (k r) -> p k r", k=KEXP),
                hps.rearrange("p (k r) -> p k r", k=KEXP),
                eg[:].unsqueeze(2).broadcast_to((128, KEXP, RANK)),
                AluOpType.mult), waits=[(TD, hp_transp_mark[t % 2])], inc=VD)
            return cv

        def ffn_a(x_src):
            wA = [(TD, p.v(TD)), (VD, p.v(VD)), (SD, p.v(SD))]
            la1 = dma("gpsimd", None, a_v[:, 0:16, :],
                      a_fm[0:2048, :].rearrange("(i p) k -> p i k", p=128), waits=wA)
            la2 = dma("gpsimd", None, a_v[:, 16:32, :],
                      a_fm[2048:4096, :].rearrange("(i p) k -> p i k", p=128), waits=wA)
            lg_w = dma("gpsimd", None, wr_sb[:],
                       wr_fm.rearrange("(i p) k -> p i k", p=128), waits=wA)
            hps2 = [ps_med[:, 0:1024], ps_big[:, 0:1024]]
            rps2 = [ps_r[:, 0:KEXP], ps_big[:, 1024:1024 + KEXP]]
            xb3 = [xbuf2[0], xbuf2[1], xbuf3]
            ln_mark = [0, 0, 0]
            for t in range(NT + 1):
                if t < NT:
                    pt = t % 2
                    px3 = t % 3
                    lx = dma("sync", None, xb3[px3], x_src[128 * t:128 * (t + 1), :],
                             waits=[(VD, ln_mark[px3])])
                    p.do("vector", lambda e: e.engine_nop(), waits=[lx], inc=VD)
                    cv = ln_tile(xb3[px3], xn_bf)
                    ln_mark[px3] = cv
                    csp = transpose_128(xn_bf, 32,
                                        lambda j0, jn, pt=pt: xnf2[pt][:, j0:j0 + jn, :],
                                        [(VD, cv)], evac="alt", px2=px_big3)
                    xn_mark[0] = p.v(TD)
                    for i in range(32):
                        wt = []
                        if i == 0:
                            wt = [(SD, csp[0]), (VD, max(csp[1], p.v(VD)))]
                            if t == 0:
                                wt += [lg_w, la1]
                        if i == 16 and t == 0:
                            wt = [la2]
                        p.do("tensor", lambda e, i=i, pt=pt: e.matmul(
                            hps2[pt][:, 0:512], xnf2[pt][:, i, :], a_v[:, i, 0:512],
                            start=(i == 0), stop=(i == 31)), waits=wt, inc=TD)
                        p.do("tensor", lambda e, i=i, pt=pt: e.matmul(
                            hps2[pt][:, 512:1024], xnf2[pt][:, i, :], a_v[:, i, 512:1024],
                            start=(i == 0), stop=(i == 31)), inc=TD)
                        p.do("tensor", lambda e, i=i, pt=pt: e.matmul(
                            rps2[pt], xnf2[pt][:, i, :], wr_sb[:, i, :],
                            start=(i == 0), stop=(i == 31)), inc=TD)
                    mm_mark[t] = p.v(TD)
                if t >= 1:
                    tp = t - 1
                    cv = ffn_gate(tp, hps2[tp % 2], rps2[tp % 2])
                    hpb = hp_bf if tp % 2 == 0 else hp_bf2
                    transpose_128(hpb, 8,
                                  lambda j0, jn, tp=tp: hpf_v[:, j0:j0 + jn, 128 * tp:128 * (tp + 1)],
                                  [(VD, cv)], evac="scalar", px2=px_big3)
                    hp_transp_mark[tp % 2] = p.v(TD)

        def ffn_b(x_src, dst, do_ln):
            lg_w = dma("gpsimd", None, bt_v, b_t.rearrange("(j p) d -> p j d", p=128),
                       waits=[(TD, p.v(TD)), (VD, p.v(VD)), (SD, p.v(SD))])
            regs = [ps_big[:, 0:1024], ps_big[:, 1024:2048]]
            xb3 = [xbuf2[0], xbuf2[1], xbuf3]
            stt_mark = [0, 0]
            free_mark = [0, 0, 0]
            store_mark = [[], [], []]
            for t in range(NT + 1):
                if t < NT:
                    pt = t % 3
                    lx = dma("sync", None, xb3[pt], x_src[128 * t:128 * (t + 1), :],
                             waits=[(VD, free_mark[pt])] + store_mark[pt])
                    for q in range(4):
                        pq = q % 2
                        for j in range(8):
                            wt = []
                            if j == 0:
                                wt = [(VD, stt_mark[pq])]
                                if t == 0 and q == 0:
                                    wt.append(lg_w)
                            for f in range(2):
                                p.do("tensor", lambda e, j=j, f=f, t=t, q=q, pq=pq: e.matmul(
                                    regs[pq][:, 512 * f:512 * (f + 1)],
                                    hpf_v[:, j, 128 * t:128 * (t + 1)],
                                    bt_v[:, j, 1024 * q + 512 * f:1024 * q + 512 * (f + 1)],
                                    start=(j == 0), stop=(j == 7)),
                                    waits=(wt if f == 0 else []), inc=TD)
                        ct = p.v(TD)
                        wl = [(TD, ct), lx]
                        for f in range(2):
                            p.do("vector", lambda e, f=f, q=q, t=t, pq=pq: e.scalar_tensor_tensor(
                                xb3[t % 3][:, 1024 * q + 512 * f:1024 * q + 512 * (f + 1)],
                                regs[pq][:, 512 * f:512 * (f + 1)],
                                rz_sb[:, t:t + 1],
                                xb3[t % 3][:, 1024 * q + 512 * f:1024 * q + 512 * (f + 1)],
                                AluOpType.mult, AluOpType.add),
                                waits=(wl if f == 0 else []), inc=VD)
                        stt_mark[pq] = p.v(VD)
                    cv = p.v(VD)
                    store_mark[pt] = [dma("sync", None, dst[128 * t:128 * (t + 1), :],
                                          xb3[pt], waits=[(VD, cv)])]
                if do_ln and t == 5:
                    cc_marks["ag1a"] = p.do("gpsimd", lambda e: e.collective_compute(
                        "AllGather", mybir.AluOpType.bypass, replica_groups=groups,
                        ins=[agin[0].opt()], outs=[agout[0].opt()]),
                        waits=ag_half_mark[0] + [(CC, p.v(CC))], inc=CC)
                if t >= 1 and do_ln:
                    tp = t - 1
                    cv = ln_tile(xb3[tp % 3], xn_bf)
                    free_mark[tp % 3] = cv
                    csp = transpose_128(xn_bf, 32,
                                         lambda j0, jn, tp=tp: xnf2[tp % 2][:, j0:j0 + jn, :],
                                         [(VD, cv)], copy_waits=fence("sync"),
                                         evac="alt", px2=px_med)
                    xn_mark[0] = p.v(TD)
                    dma("sync", None,
                        agin[tp // 4][0, :, :, 128 * (tp % 4):128 * (tp % 4 + 1)].rearrange("c p t -> p c t"),
                        xnf2[tp % 2], waits=[(SD, csp[0]), (VD, csp[1])])
                    if tp == 3:
                        ag_half_mark[0] = fence("sync")
                elif t >= 1:
                    free_mark[(t - 1) % 3] = p.v(VD)

        ffn_a(x_in)
        p.barrier()
        ffn_b(x_in, sc_x1, True)
        p.barrier()

        cc1 = p.do("gpsimd", lambda e: e.collective_compute(
            "AllGather", mybir.AluOpType.bypass, replica_groups=groups,
            ins=[agin[1].opt()], outs=[agout[1].opt()]),
            waits=fence("sync") + [(CC, p.v(CC))], inc=CC)

        xl_tok = {0: [], 1: []}
        for r in range(2):
            for lo in range(2):
                view = xn1a_v if lo == 0 else xn1b_v
                for u in range(2):
                    ccw = cc_marks["ag1a"] if u == 0 else cc1
                    xl_tok[u].append(dma(
                        "sync", None,
                        view[:, :, 1024 * r + 512 * u:1024 * r + 512 * (u + 1)],
                        agout[u][r, 16 * lo:16 * (lo + 1), :, :].rearrange("c p t -> p c t"),
                        waits=[(CC, ccw)]))
        tg_order = [0, 2, 1, 3]
        tg_w = {0: xl_tok[0], 2: xl_tok[0], 1: xl_tok[1], 3: xl_tok[1]}

        projs = [(sc_q, bqs_sb, QSCALE), (sc_k, bk_sb, 1.0), (sc_v, bv_sb, 1.0)]
        slot_mark = [0, 0]
        qe_mark = [[], []]
        v_mark = []
        for pj in range(3):
            for h in range(HL):
                k_ = pj * HL + h
                slot = k_ % 2
                wb = w_buf[slot]
                qe = qe_buf[slot]
                lw = dma("gpsimd", None, wb,
                         wqkv[pj, h].rearrange("(i p) f -> p i f", p=128),
                         waits=[(TD, slot_mark[slot])])
                for i in range(32):
                    lo, iv = (0, i) if i < 16 else (1, i - 16)
                    view = xn1a_v if lo == 0 else xn1b_v
                    for tg in tg_order:
                        wt = []
                        if i == 0:
                            wt = [lw, (SD, p.v(SD))] + tg_w[tg]
                        p.do("tensor", lambda e, i=i, tg=tg, view=view, iv=iv, wb=wb: e.matmul(
                            ps_big[:, 512 * tg:512 * (tg + 1)],
                            wb[:, i, :], view[:, iv, 512 * tg:512 * (tg + 1)],
                            start=(i == 0), stop=(i == 31)),
                            waits=wt, inc=TD)
                slot_mark[slot] = p.v(TD)
                ct = p.v(TD)
                dst, bias, scale = projs[pj]
                cs = p.do("scalar", lambda e, bias=bias, scale=scale, h=h, qe=qe: e.activation(
                    qe, ps_big[:], AF.Identity, bias=bias[:, h:h + 1], scale=scale),
                    waits=[(TD, ct)] + qe_mark[slot], inc=SD)
                if pj < 2:
                    qe_mark[slot] = [dma("sync", None, dst[h], qe, waits=[(SD, cs)])]
                else:
                    csp2 = transpose_128(qe, 16, lambda j0, jn: v_buf[:, j0:j0 + jn, :],
                                          [(SD, cs)], copy_waits=v_mark,
                                          evac="alt", px2=px_med)
                    v_mark = [dma("sync", None,
                                  sc_v[:, :, h, :].rearrange("c p d -> p c d"),
                                  v_buf, waits=[(SD, csp2[0]), (VD, csp2[1])])]
                    qe_mark[slot] = v_mark
        p.barrier()

        exp_mark = [0, 0, 0, 0]
        pair_rot = [0]
        negm_mark = [0, 0]
        ld_mark = [[], []]
        cevac_mark = 0
        cstore_mark = [[], []]

        def p4_loads(h):
            par = h % 2
            wt = [(TD, p.v(TD)), (SD, p.v(SD)), (VD, p.v(VD))]
            t1_ = dma("sync", None, k_sb2[par][:], sc_k[h], waits=wt)
            t2_ = dma("sync", None, q_sb2[par][:], sc_q[h], waits=wt)
            t3_ = dma("sync", None, v_ts2[par],
                      sc_v[:, :, h, :].rearrange("c p d -> p c d"), waits=wt)
            ld_mark[par] = [t1_, t2_, t3_]

        def p4_alk(h):
            lk = dma("gpsimd", None, alk_sb[:], alk3[:, h, :], waits=[(TD, p.v(TD))])
            return lk

        def p4_scores(h, t, lk_a):
            par = h % 2
            nk = 128 * (t + 1)
            nkb = (nk + 511) // 512
            j0w = max(0, 128 * t - WINS[h]) // 512
            live = nkb - j0w
            if live <= 2:
                base = pair_rot[0]
                pair_rot[0] = 2 - pair_rot[0]
                pos = {j: base + (j - j0w) for j in range(j0w, nkb)}
            else:
                pos = {j: j - j0w for j in range(j0w, nkb)}
            blk_ct = {}
            for j in range(j0w, nkb):
                w = min(512, nk - 512 * j)
                wt = [(SD, exp_mark[pos[j]])]
                if j == j0w:
                    wt += ld_mark[par] + [lk_a]
                p.do("tensor", lambda e, j=j, w=w, t=t, par=par, pj_=pos[j]: e.matmul(
                    ps_big[:, 512 * pj_:512 * pj_ + w],
                    q_sb2[par][:, 128 * t:128 * (t + 1)],
                    k_sb2[par][:, 512 * j:512 * j + w],
                    start=True, stop=False), waits=wt, inc=TD)
                p.do("tensor", lambda e, j=j, w=w, pj_=pos[j]: e.matmul(
                    ps_big[:, 512 * pj_:512 * pj_ + w],
                    ones3_sb[:], alk_sb[:, 512 * j:512 * j + w],
                    start=False, stop=True), inc=TD)
                blk_ct[j] = p.v(TD)
            dj = t // 4
            for j in range(j0w, nkb):
                w = min(512, nk - 512 * j)
                if j == dj:
                    p.do("vector", lambda e, t=t, pj_=pos[j]: e.tensor_tensor(
                        ps_big[:, 512 * pj_ + 128 * (t % 4):512 * pj_ + 128 * (t % 4) + 128],
                        ps_big[:, 512 * pj_ + 128 * (t % 4):512 * pj_ + 128 * (t % 4) + 128],
                        tri_sb[:], AluOpType.add), waits=[(TD, blk_ct[j])], inc=VD)
                    p.do("vector", lambda e: e.drain())
                p.do("vector", lambda e, j=j, w=w, s_=j - j0w, pj_=pos[j]: e.reduce_max(
                    bm_sb[:, s_:s_ + 1], ps_big[:, 512 * pj_:512 * pj_ + w], axis=AX),
                    waits=([(TD, blk_ct[j])] if j != dj else []), inc=VD)
            return (j0w, nkb, pos)

        def p4_combine(h, t, jb):
            j0w, nkb, pos = jb
            tp = t % 2
            p.do("vector", lambda e: e.drain(), waits=[(SD, negm_mark[tp])])
            cv = p.do("vector", lambda e, live=nkb - j0w, tp=tp: e.reduce_max(
                negm2[:, tp:tp + 1], bm_sb[:, 0:live], axis=AX, negate=True), inc=VD)
            return cv

        def p4_exp(h, t, jb, cv):
            j0w, nkb, pos = jb
            nk = 128 * (t + 1)
            tp = t % 2
            e_sb = e_sb2[tp]
            for j in range(j0w, nkb):
                w = min(512, nk - 512 * j)
                exp_mark[pos[j]] = p.do("scalar", lambda e, j=j, w=w, e_sb=e_sb, tp=tp,
                                        s_=j - j0w, pj_=pos[j]: e.activation(
                    e_sb[:, 512 * j:512 * j + w], ps_big[:, 512 * pj_:512 * pj_ + w],
                    AF.Exp, bias=negm2[:, tp:tp + 1], scale=1.0,
                    accum_out=zb4[:, s_:s_ + 1]),
                    waits=([(VD, cv)] if j == j0w else []), inc=SD)
            negm_mark[tp] = p.v(SD)

        def p4_norm(h, t, jb):
            j0w, nkb, pos = jb
            nk = 128 * (t + 1)
            tp = t % 2
            e_sb = e_sb2[tp]
            p.do("vector", lambda e, live=nkb - j0w: e.reduce_sum(
                zacc[:], zb4[:, 0:live], axis=AX),
                waits=[(SD, exp_mark[pos[nkb - 1]])], inc=VD)
            p.do("vector", lambda e: e.drain())
            p.do("vector", lambda e: e.reciprocal(rz1[:], zacc[:]), inc=VD)
            p.do("vector", lambda e: e.drain())
            cv = p.do("vector", lambda e, j0w=j0w, nk=nk, e_sb=e_sb: e.tensor_scalar(
                e_sb[:, 512 * j0w:nk], e_sb[:, 512 * j0w:nk], rz1[:], None,
                AluOpType.mult), inc=VD)
            return cv

        def p4_transp(h, t, cv_norm):
            tp = t % 2
            j0w = max(0, 128 * t - WINS[h]) // 512
            c0 = 4 * j0w
            return transpose_128(e_sb2[tp][:, 128 * c0:], t + 1 - c0,
                                 lambda j0, jn, et_sb=et_sb2[tp], c0=c0: et_sb[:, 128 * (c0 + j0):128 * (c0 + j0 + jn)],
                                 [(VD, cv_norm)], evac="alt", px2=px_med)

        def p4_pv(h, t, cse):
            nonlocal cevac_mark
            par = h % 2
            tp = t % 2
            et_sb = et_sb2[tp]
            c0 = 4 * (max(0, 128 * t - WINS[h]) // 512)
            for j2 in range(c0, t + 1):
                wt = []
                if j2 == c0:
                    wt = [(SD, max(cse[0], cevac_mark)), (VD, cse[1])]
                p.do("tensor", lambda e, j2=j2, t=t, par=par, et_sb=et_sb, c0=c0: e.matmul(
                    ps_r[:, 0:128], v_ts2[par][:, j2, :],
                    et_sb[:, 128 * j2:128 * (j2 + 1)],
                    start=(j2 == c0), stop=(j2 == t)), waits=wt, inc=TD)
            ct = p.v(TD)
            cevac_mark = p.do("scalar", lambda e, t=t, par=par: e.copy(
                ctxh_sb2[par][:, 128 * t:128 * (t + 1)], ps_r[:, 0:128]),
                waits=[(TD, ct)] + cstore_mark[par], inc=SD)

        p4_loads(0)
        lk_a = p4_alk(0)
        prev = None
        for h in range(HL):
            for t in range(16):
                jb = p4_scores(h, t, lk_a)
                cvc = p4_combine(h, t, jb)
                if prev is not None:
                    cse = p4_transp(*prev)
                p4_exp(h, t, jb, cvc)
                if prev is not None:
                    p4_pv(prev[0], prev[1], cse)
                cvn = p4_norm(h, t, jb)
                prev = (h, t, cvn)
                if t == 1 and h + 1 < HL:
                    p4_loads(h + 1)
            if h + 1 < HL:
                lk_a = p4_alk(h + 1)
            if h > 0:
                cstore_mark[(h - 1) % 2] = [dma(
                    "sync", None, cin[(h - 1) // 8][0, (h - 1) % 8],
                    ctxh_sb2[(h - 1) % 2][:], waits=[(SD, cevac_mark)])]
            if h == 9:
                cc_marks["ag2a"] = p.do("gpsimd", lambda e: e.collective_compute(
                    "AllGather", mybir.AluOpType.bypass, replica_groups=groups,
                    ins=[cin[0].opt()], outs=[cout[0].opt()]),
                    waits=fence("sync") + [(CC, p.v(CC))], inc=CC)
        cse = p4_transp(*prev)
        p4_pv(prev[0], prev[1], cse)
        cstore_mark[(HL - 1) % 2] = [dma(
            "sync", None, cin[1][0, 7], ctxh_sb2[(HL - 1) % 2][:],
            waits=[(SD, cevac_mark)])]
        p.barrier()

        cc2 = p.do("gpsimd", lambda e: e.collective_compute(
            "AllGather", mybir.AluOpType.bypass, replica_groups=groups,
            ins=[cin[1].opt()], outs=[cout[1].opt()]),
            waits=fence("sync") + [(CC, p.v(CC))], inc=CC)

        mg_done = {}
        for g in [0, 2, 1, 3]:
            r, u = g // 2, g % 2
            src = cout[u][r].rearrange("c p t -> p c t")
            wA = [(CC, cc_marks["ag2a"] if u == 0 else cc2), (VD, p.v(VD))]
            if g == 0:
                wA += [(TD, p.v(TD)), (SD, p.v(SD))]
            lA = dma("sync", None, mgA, src[:, :, 0:1024], waits=wA)
            lB = dma("sync", None, mgB, src[:, :, 1024:2048], waits=wA)
            p.do("vector", lambda e: e.tensor_tensor(
                mgB, mgB, mgA, AluOpType.subtract), waits=[lA, lB], inc=VD)
            p.do("vector", lambda e: e.drain())
            mg_done[g] = p.do("vector", lambda e, g=g: e.scalar_tensor_tensor(
                ctx_v[:, 8 * g:8 * (g + 1), :], mgB, msel_sb[:], mgA,
                AluOpType.mult, AluOpType.add), inc=VD)
            p.do("vector", lambda e: e.drain())
        iorder = list(range(0, 8)) + list(range(16, 24)) + list(range(8, 16)) + list(range(24, 32))
        cv_a = max(mg_done[0], mg_done[2])
        cv_ctx = p.v(VD)

        slot_mark2 = [p.v(TD), p.v(TD)]
        for fc in range(32):
            slot = fc % 2
            wb = w_buf[slot]
            wdma = [(TD, slot_mark2[slot])]
            if fc < 2:
                wdma.append((VD, cv_ctx))
            lw = dma("gpsimd", None, wb, wout[fc].rearrange("(i p) f -> p i f", p=128),
                     waits=wdma)
            for n_, i in enumerate(iorder):
                wt = []
                if n_ == 0:
                    wt = [lw, (VD, cv_a), (SD, p.v(SD))]
                elif n_ == 16:
                    wt = [(VD, cv_ctx)]
                for tg in range(2):
                    p.do("tensor", lambda e, i=i, tg=tg, wb=wb, n_=n_: e.matmul(
                        ps_med[:, 512 * tg:512 * (tg + 1)],
                        wb[:, i, :], ctx_v[:, i, 512 * tg:512 * (tg + 1)],
                        start=(n_ == 0), stop=(n_ == 31)),
                        waits=(wt if tg == 0 else []), inc=TD)
            slot_mark2[slot] = p.v(TD)
            ct = p.v(TD)
            cs = p.do("scalar", lambda e, fc=fc: e.activation(
                aoe_buf, ps_med[:], AF.Identity, bias=bo_sb[:, fc:fc + 1], scale=1.0),
                waits=[(TD, ct)], inc=SD)
            transpose_128(aoe_buf, 8,
                          lambda j0, jn, fc=fc: ao_v[:, j0:j0 + jn, 128 * fc:128 * (fc + 1)],
                          [(SD, cs)], evac="alt", px2=px_big3)
        p.barrier()
        for t in range(NT):
            lx = dma("sync", None, xt_p6, sc_x1[128 * t:128 * (t + 1), :],
                     waits=[(VD, p.v(VD))] + fence("sync"))
            cv = p.do("vector", lambda e, t=t: e.tensor_tensor(
                xt_p6, xt_p6, ao_v[:, t, :], AluOpType.add),
                waits=[lx], inc=VD)
            p.do("vector", lambda e: e.drain())
            dma("sync", None, sc_x2[128 * t:128 * (t + 1), :], xt_p6, waits=[(VD, cv)])
        p.barrier()

        ffn_a(sc_x2)
        p.barrier()
        ffn_b(sc_x2, out_ext, False)
        p.barrier()

        block = ctx.enter_context(nc.Block())

        def mk(name):
            def run(eng):
                for waits, fn, inc, n in p.ops[name]:
                    for s_, v_ in waits:
                        if v_ > 0:
                            eng.wait_ge(sems[s_], v_)
                    if fn is not None:
                        inst = fn(eng)
                        if inc is not None:
                            inst.then_inc(sems[inc], n)
            return run

        block.tensor(mk("tensor"))
        block.vector(mk("vector"))
        block.scalar(mk("scalar"))
        block.sync(mk("sync"))
        block.gpsimd(mk("gpsimd"))

    return nc


_NC_CACHE = {}


def _get_nc(dbg=False):
    if dbg not in _NC_CACHE:
        _NC_CACHE[dbg] = build(dbg)
    return _NC_CACHE[dbg]


def make_in_maps(L0_attn_hidden_states, L0_attn_w_qkv, L0_attn_b_qkv,
                 L0_attn_w_out, L0_attn_b_out, L0_ffn_A, L0_ffn_B, L0_ffn_Wr):
    import ml_dtypes
    x = np.ascontiguousarray(np.asarray(L0_attn_hidden_states, dtype=np.float32))
    w_qkv = np.asarray(L0_attn_w_qkv, dtype=np.float32)
    b_qkv = np.asarray(L0_attn_b_qkv, dtype=np.float32)
    w_out = np.asarray(L0_attn_w_out, dtype=np.float32)
    b_out = np.asarray(L0_attn_b_out, dtype=np.float32)
    A = np.asarray(L0_ffn_A, dtype=np.float32)
    Bm = np.asarray(L0_ffn_B, dtype=np.float32)
    Wr = np.asarray(L0_ffn_Wr, dtype=np.float32)

    slopes = get_alibi_slopes(32)
    kpos = np.arange(S, dtype=np.float32)

    a_fm = np.ascontiguousarray(A.reshape(KR, D).T)
    b_t = np.ascontiguousarray(Bm.transpose(0, 2, 1).reshape(KR, D))
    wr_fm = np.ascontiguousarray(Wr.T)
    ident = np.eye(128, dtype=np.float32)
    tri = np.where(np.arange(128)[None, :] > np.arange(128)[:, None],
                   np.float32(NEG), np.float32(0.0)).astype(np.float32)
    ones3 = np.ones((3, 128), dtype=np.float32)
    bo_col = np.ascontiguousarray(b_out.reshape(32, 128).T)

    def bf(v):
        return v.astype(ml_dtypes.bfloat16).astype(np.float32)

    perm = ([2 * i for i in range(8)] + [2 * i for i in range(8, 16)]
            + [2 * i + 1 for i in range(8)] + [2 * i + 1 for i in range(8, 16)])
    wo_p = np.ascontiguousarray(
        w_out.reshape(32, 128, D)[perm].reshape(D, D))
    wo = np.ascontiguousarray(wo_p.reshape(D, 32, 128).transpose(1, 0, 2))
    in_maps = []
    for c in range(8):
        b = c // 2
        half = c % 2
        gheads = [2 * i + half for i in range(HL)]
        wq = np.empty((3, HL, D, HD), dtype=np.float32)
        for pj in range(3):
            for h in range(HL):
                col0 = pj * D + gheads[h] * HD
                wq[pj, h] = w_qkv[:, col0:col0 + HD]
        bq_col = np.empty((128, HL), dtype=np.float32)
        bk_col = np.empty((128, HL), dtype=np.float32)
        bv_col = np.empty((128, HL), dtype=np.float32)
        for h in range(HL):
            g = gheads[h]
            bq_col[:, h] = b_qkv[0 * D + g * HD: 0 * D + g * HD + HD]
            bk_col[:, h] = b_qkv[1 * D + g * HD: 1 * D + g * HD + HD]
            bv_col[:, h] = b_qkv[2 * D + g * HD: 2 * D + g * HD + HD]
        alk3_a = np.empty((3, HL, S), dtype=np.float32)
        for h in range(HL):
            v = slopes[gheads[h]] * kpos
            hi = bf(v)
            mid = bf(v - hi)
            lo = bf(v - hi - mid)
            alk3_a[0, h], alk3_a[1, h], alk3_a[2, h] = hi, mid, lo
        in_maps.append({
            "x": np.ascontiguousarray(x[b, half * TOWN:(half + 1) * TOWN, :]),
            "wqkv": wq,
            "wout": wo,
            "a_fm": a_fm,
            "b_t": b_t,
            "wr_fm": wr_fm,
            "bq_col": bq_col,
            "bk_col": bk_col,
            "bv_col": bv_col,
            "bo_col": bo_col,
            "ident": ident,
            "tri": tri,
            "alk3": alk3_a,
            "ones3": ones3,
            "msel": np.full((128, 1), float(half), dtype=np.float32),
        })
    return in_maps


def kernel(L0_attn_hidden_states, L0_attn_w_qkv, L0_attn_b_qkv, L0_attn_w_out,
           L0_attn_b_out, L0_ffn_A, L0_ffn_B, L0_ffn_Wr):
    in_maps = make_in_maps(L0_attn_hidden_states, L0_attn_w_qkv, L0_attn_b_qkv,
                           L0_attn_w_out, L0_attn_b_out, L0_ffn_A, L0_ffn_B,
                           L0_ffn_Wr)
    nc = _get_nc(False)
    res = run_bass_kernel_spmd(nc, in_maps, core_ids=list(range(8)))
    out = np.empty((B, S, D), dtype=np.float32)
    for c in range(8):
        b = c // 2
        half = c % 2
        out[b, half * TOWN:(half + 1) * TOWN, :] = res.results[c]["out"]
    return out


# version 22
# speedup vs baseline: 1.0610x; 1.0044x over previous
import math
from contextlib import ExitStack

import numpy as np
import concourse.bass as bass
import concourse.mybir as mybir
from concourse.alu_op_type import AluOpType
from concourse.bass_utils import run_bass_kernel_spmd

F32 = mybir.dt.float32
BF16 = mybir.dt.bfloat16
AF = mybir.ActivationFunctionType
AX = mybir.AxisListType.X

B, S, D = 4, 2048, 4096
HL, HD = 16, 128
TOWN = 1024
NT = TOWN // 128
KEXP, RANK = 64, 16
KR = KEXP * RANK
QSCALE = HD ** (-0.5)
EPS = 1e-5
NEG = -1.0e9

ENGS = ("tensor", "vector", "scalar", "sync", "gpsimd")
CC, TD, VD, SD = "cc", "td", "vd", "sd"
SYNC_SEMS = ("ly0", "ly1", "ly2", "ly3")
GP_SEMS = ("lg0", "lg1")
ALL_SEMS = SYNC_SEMS + GP_SEMS + (CC, TD, VD, SD)


def get_alibi_slopes(n):
    cp2 = 2 ** math.floor(math.log2(n))
    base = 2 ** (-(2 ** (-(math.log2(cp2) - 3))))
    powers = np.arange(1, 1 + cp2, dtype=np.float32)
    slopes = np.power(base, powers).astype(np.float32)
    if cp2 != n:
        eb = 2 ** (-(2 ** (-(math.log2(2 * cp2) - 3))))
        ep = np.arange(1, 1 + 2 * min(cp2, n - cp2), 2, dtype=np.float32)
        slopes = np.concatenate([slopes, np.power(eb, ep).astype(np.float32)])
    return slopes


class Plan:
    def __init__(self):
        self.ops = {k: [] for k in ENGS}
        self.cnt = {}

    def do(self, eng, fn, waits=(), inc=None, n=1):
        if inc is not None:
            self.cnt[inc] = self.cnt.get(inc, 0) + n
        self.ops[eng].append((tuple(waits), fn, inc, n))
        return self.cnt.get(inc, 0) if inc is not None else None

    def v(self, sem):
        return self.cnt.get(sem, 0)

    def barrier(self):
        snap = tuple((k, v) for k, v in self.cnt.items() if v > 0)
        for eng in ENGS:
            self.ops[eng].append((snap, None, None, 0))


WIN_MARGIN = 164.0


def local_windows():
    slopes = get_alibi_slopes(32)
    wins = []
    for i in range(HL):
        sl = min(slopes[2 * i], slopes[2 * i + 1])
        w = int(math.ceil(WIN_MARGIN / sl))
        wins.append(min(w, S))
    return wins


def build(dbg=False):
    nc = bass.Bass("TRN2", target_bir_lowering=False)
    WINS = local_windows()

    def inp(name, shape):
        return nc.dram_tensor(name, shape, F32, kind="ExternalInput").ap()

    x_in = inp("x", [TOWN, D])
    wqkv = inp("wqkv", [3, HL, D, HD])
    wout = inp("wout", [32, D, 128])
    a_fm = inp("a_fm", [D, KR])
    b_t = inp("b_t", [KR, D])
    wr_fm = inp("wr_fm", [D, KEXP])
    bq_col = inp("bq_col", [128, HL])
    bk_col = inp("bk_col", [128, HL])
    bv_col = inp("bv_col", [128, HL])
    bo_col = inp("bo_col", [128, 32])
    ident = inp("ident", [128, 128])
    tri = inp("tri", [128, 128])
    alk3 = inp("alk3", [3, HL, S])
    ones3 = inp("ones3", [3, 128])
    msel = inp("msel", [128, 1])

    out_ext = nc.dram_tensor("out", [TOWN, D], F32, kind="ExternalOutput").ap()

    def scratch(name, shape, dt):
        kind = "ExternalOutput" if dbg else "Internal"
        return nc.dram_tensor(name, shape, dt, kind=kind).ap()

    sc_x1 = scratch("sc_x1", [TOWN, D], F32)
    sc_x2 = scratch("sc_x2", [TOWN, D], F32)
    sc_q = scratch("sc_q", [HL, 128, S], BF16)
    sc_k = scratch("sc_k", [HL, 128, S], BF16)
    sc_v = scratch("sc_v", [16, 128, HL, 128], BF16)
    agin = [nc.dram_tensor("agin0", [1, 32, 128, 512], BF16).ap(),
            nc.dram_tensor("agin1", [1, 32, 128, 512], BF16).ap()]
    agout = [nc.dram_tensor("agout0", [2, 32, 128, 512], BF16).ap(),
             nc.dram_tensor("agout1", [2, 32, 128, 512], BF16).ap()]
    cin = [nc.dram_tensor("cin0", [1, 8, 128, S], BF16).ap(),
           nc.dram_tensor("cin1", [1, 8, 128, S], BF16).ap()]
    cout = [nc.dram_tensor("cout0", [2, 8, 128, S], BF16).ap(),
            nc.dram_tensor("cout1", [2, 8, 128, S], BF16).ap()]
    groups = [[0, 1], [2, 3], [4, 5], [6, 7]]

    p = Plan()

    with ExitStack() as ctx:
        def sb(name, shape, dt):
            return ctx.enter_context(nc.sbuf_tensor(name, shape, dt))

        sems = {}
        for s_ in ALL_SEMS:
            sems[s_] = ctx.enter_context(nc.semaphore(s_))

        ident_sb = sb("ident_sb", [128, 128], BF16)
        tri_sb = sb("tri_sb", [128, 128], F32)
        ones3_sb = sb("ones3_sb", [3, 128], BF16)
        alk_sb = sb("alk_sb", [3, S], BF16)
        bq_sb = sb("bq_sb", [128, HL], F32)
        bk_sb = sb("bk_sb", [128, HL], F32)
        bv_sb = sb("bv_sb", [128, HL], F32)
        bo_sb = sb("bo_sb", [128, 32], F32)
        bqs_sb = sb("bqs_sb", [128, HL], F32)
        msel_sb = sb("msel_sb", [128, 1], F32)

        bnst = sb("bnst", [128, 48], F32)
        mv = sb("mv", [128, 2], F32)
        t1 = sb("t1", [128, 1], F32)
        rstd = sb("rstd", [128, 1], F32)
        negm = sb("negm", [128, 1], F32)
        negm2 = sb("negm2", [128, 2], F32)
        zacc = sb("zacc", [128, 1], F32)
        rz1 = sb("rz1", [128, 1], F32)
        scr = sb("scr", [128, KEXP], F32)
        top8a = sb("top8a", [128, 8], F32)
        top8b = sb("top8b", [128, 8], F32)
        rep = sb("rep", [128, KEXP], F32)
        e_r = sb("e_r", [128, KEXP], F32)
        selm = sb("selm", [128, KEXP], F32)
        eg = sb("eg", [128, KEXP], F32)
        zg = sb("zg", [128, 1], F32)
        rz_sb = sb("rz_sb", [128, NT], F32)

        ar1 = sb("ar1", [128, 32768], BF16)
        ar2 = sb("ar2", [128, 32768], BF16)
        pool = sb("pool", [128, 16384], BF16)
        wr_sb = sb("wr_sb", [128, 32, KEXP], BF16)
        xn_bf = sb("xn_bf", [128, D], BF16)
        hp_bf = sb("hp_bf", [128, KR], BF16)
        hp_bf2 = sb("hp_bf2", [128, KR], BF16)

        a_v = ar1[:].rearrange("p (i k) -> p i k", i=32)
        bt_v = ar1[:].rearrange("p (j d) -> p j d", j=8)
        ao_v = ar1[:].rearrange("p (t f) -> p t f", t=8)
        xn1a_v = ar1[:].rearrange("p (i t) -> p i t", i=16)
        xn1b_v = ar2[:].rearrange("p (i t) -> p i t", i=16)
        ctx_v = ar2[:].rearrange("p (i t) -> p i t", i=32)

        ar2f = ar2[:].bitcast(F32)
        hpf_v = ar2[:, 0:8192].rearrange("p (j t) -> p j t", j=8)
        xnf2 = [ar2[:, 8192:12288].rearrange("p (i t) -> p i t", i=32),
                ar2[:, 12288:16384].rearrange("p (i t) -> p i t", i=32)]
        xbuf2 = [ar2f[:, 8192:12288], ar2f[:, 12288:16384]]

        poolf = pool[:].bitcast(F32)
        xbuf3 = poolf[:, 0:4096]
        w_buf = [pool[:, 0:4096].rearrange("p (i f) -> p i f", i=32),
                 pool[:, 4096:8192].rearrange("p (i f) -> p i f", i=32)]
        qe_buf = [pool[:, 8192:10240], pool[:, 10240:12288]]
        v_buf = pool[:, 12288:14336].rearrange("p (c f) -> p c f", c=16)
        k_sb2 = [ar2[:, 0:2048], ar2[:, 2048:4096]]
        q_sb2 = [ar2[:, 4096:6144], ar2[:, 6144:8192]]
        v_ts2 = [ar2[:, 8192:10240].rearrange("p (c f) -> p c f", c=16),
                 ar2[:, 10240:12288].rearrange("p (c f) -> p c f", c=16)]
        ctxh_sb2 = [ar2[:, 12288:14336], ar2[:, 14336:16384]]
        e_sb2 = [ar2[:, 16384:18432], ar2[:, 18432:20480]]
        et_sb2 = [ar2[:, 20480:22528], ar2[:, 22528:24576]]
        bm_sb = sb("bm_sb", [128, 4], F32)
        zb4 = sb("zb4", [128, 4], F32)
        mgA = pool[:, 0:8192].rearrange("p (c t) -> p c t", c=8)
        mgB = pool[:, 8192:16384].rearrange("p (c t) -> p c t", c=8)
        aoe_buf = pool[:, 8192:9216]
        xt_p6 = poolf[:, 0:4096]

        ps_big = ctx.enter_context(nc.psum_tensor("ps_big", [128, 2048], F32))
        ps_med = ctx.enter_context(nc.psum_tensor("ps_med", [128, 1024], F32))
        ps_r = ctx.enter_context(nc.psum_tensor("ps_r", [128, 512], F32))
        ps_x = ctx.enter_context(nc.psum_tensor("ps_x", [128, 1024], BF16))
        px_med = ps_med[:, 0:512].bitcast(BF16)
        px_big3 = ps_big[:, 1536:2048].bitcast(BF16)

        mm_mark = {}
        hp_transp_mark = [0, 0]
        xn_mark = [0]
        ag_half_mark = [[]]
        cc_marks = {}

        rot = {"sync": list(SYNC_SEMS), "gpsimd": list(GP_SEMS)}
        rix = {"sync": 0, "gpsimd": 0}

        def dma(eng, _sem, out, in_, waits=()):
            i = rix[eng]
            rix[eng] = (i + 1) % len(rot[eng])
            s_ = rot[eng][i]
            wt = [(s_, p.v(s_))] + list(waits)
            c = p.do(eng, lambda e, o=out, i_=in_: e.dma_start(out=o, in_=i_),
                     waits=wt, inc=s_, n=16)
            return (s_, c)

        def fence(eng):
            return [(s_, p.v(s_)) for s_ in rot[eng]]

        def fence_all():
            return fence("sync") + fence("gpsimd")

        dma("gpsimd", None, ident_sb[:], ident)
        dma("gpsimd", None, ones3_sb[:], ones3)
        dma("sync", None, tri_sb[:], tri)
        dma("sync", None, bq_sb[:], bq_col)
        dma("sync", None, bk_sb[:], bk_col)
        dma("sync", None, bv_sb[:], bv_col)
        dma("sync", None, bo_sb[:], bo_col)
        dma("sync", None, msel_sb[:], msel)
        p.do("vector", lambda e: e.tensor_scalar(
            bqs_sb[:], bq_sb[:], QSCALE, None, AluOpType.mult),
            waits=fence("sync"), inc=VD)
        p.barrier()

        def ln_tile(x_ap, out_bf):
            p.do("vector", lambda e: e.drain())
            for i in range(8):
                p.do("vector", lambda e, i=i: e.bn_stats(
                    bnst[:, 6 * i:6 * i + 6], x_ap[:, 512 * i:512 * (i + 1)]), inc=VD)
            p.do("vector", lambda e: e.drain())
            p.do("vector", lambda e: e.bn_aggr(mv[:], bnst[:]), inc=VD)
            p.do("vector", lambda e: e.drain())
            cv = p.do("vector", lambda e: e.tensor_scalar(
                t1[:], mv[:, 1:2], EPS, None, AluOpType.add), inc=VD)
            cs = p.do("scalar", lambda e: e.activation(rstd[:], t1[:], AF.Sqrt),
                      waits=[(VD, cv)], inc=SD)
            p.do("vector", lambda e: e.reciprocal(rstd[:], rstd[:]),
                 waits=[(SD, cs)], inc=VD)
            p.do("vector", lambda e: e.drain())
            cv = p.do("vector", lambda e: e.tensor_scalar(
                out_bf[:], x_ap[:], mv[:, 0:1], rstd[:],
                AluOpType.subtract, AluOpType.mult),
                waits=[(TD, xn_mark[0])], inc=VD)
            return cv

        def transpose_128(src_bf, n128, dst_fn, waits0, copy_waits=(), evac="scalar",
                          px2=None):
            cs_pair = [p.v(SD), p.v(VD)]
            cw_done = set()
            PK = 8 if px2 is not None else 4
            rmark = {0: [p.v(SD), p.v(VD)], 1: [p.v(SD), p.v(VD)]}
            for j0 in range(0, n128, PK):
                jn = min(PK, n128 - j0)
                pk = (j0 // PK)
                reg = pk % 2 if px2 is not None else 0
                px = ps_x if reg == 0 else px2
                if evac == "alt":
                    ev = "scalar" if pk % 2 == 0 else "vector"
                else:
                    ev = evac
                for j in range(j0, j0 + jn):
                    wt = []
                    if j == j0:
                        wt = [(SD, rmark[reg][0]), (VD, rmark[reg][1])]
                    if j == 0:
                        wt = wt + list(waits0)
                    p.do("tensor", lambda e, j=j, px=px, PK=PK: e.transpose(
                        px[:, 128 * (j % PK):128 * (j % PK) + 128],
                        src_bf[:, 128 * j:128 * j + 128], ident_sb[:]),
                        waits=wt, inc=TD)
                ct = p.v(TD)
                dst = dst_fn(j0, jn)
                src = px[:, 0:128 * jn]
                if len(dst.shape) == 3:
                    src = src.rearrange("p (j f) -> p j f", j=jn)
                wt = [(TD, ct)]
                if ev not in cw_done:
                    wt += list(copy_waits)
                    cw_done.add(ev)
                if ev == "scalar":
                    cs = p.do("scalar", lambda e, dst=dst, src=src: e.copy(dst, src),
                              waits=wt, inc=SD)
                    rmark[reg][0] = cs
                    cs_pair[0] = cs
                else:
                    cs = p.do("vector", lambda e, dst=dst, src=src: e.tensor_copy(dst, src),
                              waits=wt, inc=VD)
                    rmark[reg][1] = cs
                    cs_pair[1] = cs
            return cs_pair

        def ffn_gate(t, hps, rps):
            csc = p.do("scalar", lambda e, rps=rps: e.copy(scr[:], rps),
                       waits=[(TD, mm_mark[t]), (VD, p.v(VD))], inc=SD)
            p.do("vector", lambda e: e.max(top8a[:], scr[:]),
                 waits=[(SD, csc)], inc=VD)
            p.do("vector", lambda e: e.drain())
            p.do("vector", lambda e: e.match_replace(rep[:], top8a[:], scr[:], -3.0e38), inc=VD)
            p.do("vector", lambda e: e.drain())
            p.do("vector", lambda e: e.max(top8b[:], rep[:]), inc=VD)
            p.do("vector", lambda e: e.drain())
            cv = p.do("vector", lambda e: e.tensor_scalar(
                negm[:], top8a[:, 0:1], -1.0, None, AluOpType.mult), inc=VD)
            p.do("vector", lambda e: e.drain())
            cs2 = p.do("scalar", lambda e: e.activation(
                e_r[:], scr[:], AF.Exp, bias=negm[:], scale=1.0),
                waits=[(VD, cv)], inc=SD)
            p.do("vector", lambda e: e.scalar_tensor_tensor(
                eg[:], scr[:], top8b[:, 7:8], e_r[:],
                AluOpType.is_ge, AluOpType.mult),
                waits=[(SD, cs2)], inc=VD)
            p.do("vector", lambda e: e.drain())
            p.do("vector", lambda e: e.reduce_sum(zg[:], eg[:], axis=AX), inc=VD)
            p.do("vector", lambda e: e.drain())
            p.do("vector", lambda e, t=t: e.reciprocal(rz_sb[:, t:t + 1], zg[:]), inc=VD)
            hpb = hp_bf if t % 2 == 0 else hp_bf2
            cv = p.do("vector", lambda e, hps=hps, hpb=hpb: e.tensor_tensor(
                hpb[:].rearrange("p (k r) -> p k r", k=KEXP),
                hps.rearrange("p (k r) -> p k r", k=KEXP),
                eg[:].unsqueeze(2).broadcast_to((128, KEXP, RANK)),
                AluOpType.mult), waits=[(TD, hp_transp_mark[t % 2])], inc=VD)
            return cv

        def ffn_a(x_src):
            wA = [(TD, p.v(TD)), (VD, p.v(VD)), (SD, p.v(SD))]
            la1 = dma("gpsimd", None, a_v[:, 0:16, :],
                      a_fm[0:2048, :].rearrange("(i p) k -> p i k", p=128), waits=wA)
            la2 = dma("gpsimd", None, a_v[:, 16:32, :],
                      a_fm[2048:4096, :].rearrange("(i p) k -> p i k", p=128), waits=wA)
            lg_w = dma("gpsimd", None, wr_sb[:],
                       wr_fm.rearrange("(i p) k -> p i k", p=128), waits=wA)
            hps2 = [ps_med[:, 0:1024], ps_big[:, 0:1024]]
            rps2 = [ps_r[:, 0:KEXP], ps_big[:, 1024:1024 + KEXP]]
            xb3 = [xbuf2[0], xbuf2[1], xbuf3]
            ln_mark = [0, 0, 0]
            for t in range(NT + 1):
                if t < NT:
                    pt = t % 2
                    px3 = t % 3
                    lx = dma("sync", None, xb3[px3], x_src[128 * t:128 * (t + 1), :],
                             waits=[(VD, ln_mark[px3])])
                    p.do("vector", lambda e: e.engine_nop(), waits=[lx], inc=VD)
                    cv = ln_tile(xb3[px3], xn_bf)
                    ln_mark[px3] = cv
                    csp = transpose_128(xn_bf, 32,
                                        lambda j0, jn, pt=pt: xnf2[pt][:, j0:j0 + jn, :],
                                        [(VD, cv)], evac="alt", px2=px_big3)
                    xn_mark[0] = p.v(TD)
                    for i in range(32):
                        wt = []
                        if i == 0:
                            wt = [(SD, csp[0]), (VD, max(csp[1], p.v(VD)))]
                            if t == 0:
                                wt += [lg_w, la1]
                        if i == 16 and t == 0:
                            wt = [la2]
                        p.do("tensor", lambda e, i=i, pt=pt: e.matmul(
                            hps2[pt][:, 0:512], xnf2[pt][:, i, :], a_v[:, i, 0:512],
                            start=(i == 0), stop=(i == 31)), waits=wt, inc=TD)
                        p.do("tensor", lambda e, i=i, pt=pt: e.matmul(
                            hps2[pt][:, 512:1024], xnf2[pt][:, i, :], a_v[:, i, 512:1024],
                            start=(i == 0), stop=(i == 31)), inc=TD)
                        p.do("tensor", lambda e, i=i, pt=pt: e.matmul(
                            rps2[pt], xnf2[pt][:, i, :], wr_sb[:, i, :],
                            start=(i == 0), stop=(i == 31)), inc=TD)
                    mm_mark[t] = p.v(TD)
                if t >= 1:
                    tp = t - 1
                    cv = ffn_gate(tp, hps2[tp % 2], rps2[tp % 2])
                    hpb = hp_bf if tp % 2 == 0 else hp_bf2
                    transpose_128(hpb, 8,
                                  lambda j0, jn, tp=tp: hpf_v[:, j0:j0 + jn, 128 * tp:128 * (tp + 1)],
                                  [(VD, cv)], evac="scalar", px2=px_big3)
                    hp_transp_mark[tp % 2] = p.v(TD)

        def ffn_b(x_src, dst, do_ln):
            lg_w = dma("gpsimd", None, bt_v, b_t.rearrange("(j p) d -> p j d", p=128),
                       waits=[(TD, p.v(TD)), (VD, p.v(VD)), (SD, p.v(SD))])
            regs = [ps_big[:, 0:1024], ps_big[:, 1024:2048]]
            xb3 = [xbuf2[0], xbuf2[1], xbuf3]
            stt_mark = [0, 0]
            free_mark = [0, 0, 0]
            store_mark = [[], [], []]
            for t in range(NT + 1):
                if t < NT:
                    pt = t % 3
                    lx = dma("sync", None, xb3[pt], x_src[128 * t:128 * (t + 1), :],
                             waits=[(VD, free_mark[pt])] + store_mark[pt])
                    for q in range(4):
                        pq = q % 2
                        for j in range(8):
                            wt = []
                            if j == 0:
                                wt = [(VD, stt_mark[pq])]
                                if t == 0 and q == 0:
                                    wt.append(lg_w)
                            for f in range(2):
                                p.do("tensor", lambda e, j=j, f=f, t=t, q=q, pq=pq: e.matmul(
                                    regs[pq][:, 512 * f:512 * (f + 1)],
                                    hpf_v[:, j, 128 * t:128 * (t + 1)],
                                    bt_v[:, j, 1024 * q + 512 * f:1024 * q + 512 * (f + 1)],
                                    start=(j == 0), stop=(j == 7)),
                                    waits=(wt if f == 0 else []), inc=TD)
                        ct = p.v(TD)
                        wl = [(TD, ct), lx]
                        for f in range(2):
                            p.do("vector", lambda e, f=f, q=q, t=t, pq=pq: e.scalar_tensor_tensor(
                                xb3[t % 3][:, 1024 * q + 512 * f:1024 * q + 512 * (f + 1)],
                                regs[pq][:, 512 * f:512 * (f + 1)],
                                rz_sb[:, t:t + 1],
                                xb3[t % 3][:, 1024 * q + 512 * f:1024 * q + 512 * (f + 1)],
                                AluOpType.mult, AluOpType.add),
                                waits=(wl if f == 0 else []), inc=VD)
                        stt_mark[pq] = p.v(VD)
                    cv = p.v(VD)
                    store_mark[pt] = [dma("sync", None, dst[128 * t:128 * (t + 1), :],
                                          xb3[pt], waits=[(VD, cv)])]
                if do_ln and t == 5:
                    cc_marks["ag1a"] = p.do("gpsimd", lambda e: e.collective_compute(
                        "AllGather", mybir.AluOpType.bypass, replica_groups=groups,
                        ins=[agin[0].opt()], outs=[agout[0].opt()]),
                        waits=ag_half_mark[0] + [(CC, p.v(CC))], inc=CC)
                if t >= 1 and do_ln:
                    tp = t - 1
                    cv = ln_tile(xb3[tp % 3], xn_bf)
                    free_mark[tp % 3] = cv
                    csp = transpose_128(xn_bf, 32,
                                         lambda j0, jn, tp=tp: xnf2[tp % 2][:, j0:j0 + jn, :],
                                         [(VD, cv)], copy_waits=fence("sync"),
                                         evac="alt", px2=px_med)
                    xn_mark[0] = p.v(TD)
                    dma("sync", None,
                        agin[tp // 4][0, :, :, 128 * (tp % 4):128 * (tp % 4 + 1)].rearrange("c p t -> p c t"),
                        xnf2[tp % 2], waits=[(SD, csp[0]), (VD, csp[1])])
                    if tp == 3:
                        ag_half_mark[0] = fence("sync")
                elif t >= 1:
                    free_mark[(t - 1) % 3] = p.v(VD)

        ffn_a(x_in)
        p.barrier()
        ffn_b(x_in, sc_x1, True)
        p.barrier()

        cc1 = p.do("gpsimd", lambda e: e.collective_compute(
            "AllGather", mybir.AluOpType.bypass, replica_groups=groups,
            ins=[agin[1].opt()], outs=[agout[1].opt()]),
            waits=fence("sync") + [(CC, p.v(CC))], inc=CC)

        xl_tok = {0: [], 1: []}
        for r in range(2):
            for lo in range(2):
                view = xn1a_v if lo == 0 else xn1b_v
                for u in range(2):
                    ccw = cc_marks["ag1a"] if u == 0 else cc1
                    xl_tok[u].append(dma(
                        "sync", None,
                        view[:, :, 1024 * r + 512 * u:1024 * r + 512 * (u + 1)],
                        agout[u][r, 16 * lo:16 * (lo + 1), :, :].rearrange("c p t -> p c t"),
                        waits=[(CC, ccw)]))
        tg_order = [0, 2, 1, 3]
        tg_w = {0: xl_tok[0], 2: xl_tok[0], 1: xl_tok[1], 3: xl_tok[1]}

        projs = [(sc_q, bqs_sb, QSCALE), (sc_k, bk_sb, 1.0), (sc_v, bv_sb, 1.0)]
        slot_mark = [0, 0]
        qe_mark = [[], []]
        v_mark = []
        for pj in range(3):
            for h in range(HL):
                k_ = pj * HL + h
                slot = k_ % 2
                wb = w_buf[slot]
                qe = qe_buf[slot]
                lw = dma("gpsimd", None, wb,
                         wqkv[pj, h].rearrange("(i p) f -> p i f", p=128),
                         waits=[(TD, slot_mark[slot])])
                sweeps = [[0, 2], [1, 3]] if k_ < 2 else [tg_order]
                for tgs in sweeps:
                    for i in range(32):
                        lo, iv = (0, i) if i < 16 else (1, i - 16)
                        view = xn1a_v if lo == 0 else xn1b_v
                        for tg in tgs:
                            wt = []
                            if i == 0:
                                wt = [lw, (SD, p.v(SD))] + tg_w[tg]
                            p.do("tensor", lambda e, i=i, tg=tg, view=view, iv=iv, wb=wb: e.matmul(
                                ps_big[:, 512 * tg:512 * (tg + 1)],
                                wb[:, i, :], view[:, iv, 512 * tg:512 * (tg + 1)],
                                start=(i == 0), stop=(i == 31)),
                                waits=wt, inc=TD)
                slot_mark[slot] = p.v(TD)
                ct = p.v(TD)
                dst, bias, scale = projs[pj]
                cs = p.do("scalar", lambda e, bias=bias, scale=scale, h=h, qe=qe: e.activation(
                    qe, ps_big[:], AF.Identity, bias=bias[:, h:h + 1], scale=scale),
                    waits=[(TD, ct)] + qe_mark[slot], inc=SD)
                if pj < 2:
                    qe_mark[slot] = [dma("sync", None, dst[h], qe, waits=[(SD, cs)])]
                else:
                    csp2 = transpose_128(qe, 16, lambda j0, jn: v_buf[:, j0:j0 + jn, :],
                                          [(SD, cs)], copy_waits=v_mark,
                                          evac="alt", px2=px_med)
                    v_mark = [dma("sync", None,
                                  sc_v[:, :, h, :].rearrange("c p d -> p c d"),
                                  v_buf, waits=[(SD, csp2[0]), (VD, csp2[1])])]
                    qe_mark[slot] = v_mark
        p.barrier()

        exp_mark = [0, 0, 0, 0]
        pair_rot = [0]
        negm_mark = [0, 0]
        ld_mark = [[], []]
        cevac_mark = 0
        cstore_mark = [[], []]

        def p4_loads(h):
            par = h % 2
            wt = [(TD, p.v(TD)), (SD, p.v(SD)), (VD, p.v(VD))]
            t1_ = dma("sync", None, k_sb2[par][:], sc_k[h], waits=wt)
            t2_ = dma("sync", None, q_sb2[par][:], sc_q[h], waits=wt)
            t3_ = dma("sync", None, v_ts2[par],
                      sc_v[:, :, h, :].rearrange("c p d -> p c d"), waits=wt)
            ld_mark[par] = [t1_, t2_, t3_]

        def p4_alk(h):
            lk = dma("gpsimd", None, alk_sb[:], alk3[:, h, :], waits=[(TD, p.v(TD))])
            return lk

        def p4_scores(h, t, lk_a):
            par = h % 2
            nk = 128 * (t + 1)
            nkb = (nk + 511) // 512
            j0w = max(0, 128 * t - WINS[h]) // 512
            live = nkb - j0w
            if live <= 2:
                base = pair_rot[0]
                pair_rot[0] = 2 - pair_rot[0]
                pos = {j: base + (j - j0w) for j in range(j0w, nkb)}
            else:
                pos = {j: j - j0w for j in range(j0w, nkb)}
            blk_ct = {}
            for j in range(j0w, nkb):
                w = min(512, nk - 512 * j)
                wt = [(SD, exp_mark[pos[j]])]
                if j == j0w:
                    wt += ld_mark[par] + [lk_a]
                p.do("tensor", lambda e, j=j, w=w, t=t, par=par, pj_=pos[j]: e.matmul(
                    ps_big[:, 512 * pj_:512 * pj_ + w],
                    q_sb2[par][:, 128 * t:128 * (t + 1)],
                    k_sb2[par][:, 512 * j:512 * j + w],
                    start=True, stop=False), waits=wt, inc=TD)
                p.do("tensor", lambda e, j=j, w=w, pj_=pos[j]: e.matmul(
                    ps_big[:, 512 * pj_:512 * pj_ + w],
                    ones3_sb[:], alk_sb[:, 512 * j:512 * j + w],
                    start=False, stop=True), inc=TD)
                blk_ct[j] = p.v(TD)
            dj = t // 4
            for j in range(j0w, nkb):
                w = min(512, nk - 512 * j)
                if j == dj:
                    p.do("vector", lambda e, t=t, pj_=pos[j]: e.tensor_tensor(
                        ps_big[:, 512 * pj_ + 128 * (t % 4):512 * pj_ + 128 * (t % 4) + 128],
                        ps_big[:, 512 * pj_ + 128 * (t % 4):512 * pj_ + 128 * (t % 4) + 128],
                        tri_sb[:], AluOpType.add), waits=[(TD, blk_ct[j])], inc=VD)
                    p.do("vector", lambda e: e.drain())
                p.do("vector", lambda e, j=j, w=w, s_=j - j0w, pj_=pos[j]: e.reduce_max(
                    bm_sb[:, s_:s_ + 1], ps_big[:, 512 * pj_:512 * pj_ + w], axis=AX),
                    waits=([(TD, blk_ct[j])] if j != dj else []), inc=VD)
            return (j0w, nkb, pos)

        def p4_combine(h, t, jb):
            j0w, nkb, pos = jb
            tp = t % 2
            p.do("vector", lambda e: e.drain(), waits=[(SD, negm_mark[tp])])
            cv = p.do("vector", lambda e, live=nkb - j0w, tp=tp: e.reduce_max(
                negm2[:, tp:tp + 1], bm_sb[:, 0:live], axis=AX, negate=True), inc=VD)
            return cv

        def p4_exp(h, t, jb, cv):
            j0w, nkb, pos = jb
            nk = 128 * (t + 1)
            tp = t % 2
            e_sb = e_sb2[tp]
            for j in range(j0w, nkb):
                w = min(512, nk - 512 * j)
                exp_mark[pos[j]] = p.do("scalar", lambda e, j=j, w=w, e_sb=e_sb, tp=tp,
                                        s_=j - j0w, pj_=pos[j]: e.activation(
                    e_sb[:, 512 * j:512 * j + w], ps_big[:, 512 * pj_:512 * pj_ + w],
                    AF.Exp, bias=negm2[:, tp:tp + 1], scale=1.0,
                    accum_out=zb4[:, s_:s_ + 1]),
                    waits=([(VD, cv)] if j == j0w else []), inc=SD)
            negm_mark[tp] = p.v(SD)

        def p4_norm(h, t, jb):
            j0w, nkb, pos = jb
            nk = 128 * (t + 1)
            tp = t % 2
            e_sb = e_sb2[tp]
            p.do("vector", lambda e, live=nkb - j0w: e.reduce_sum(
                zacc[:], zb4[:, 0:live], axis=AX),
                waits=[(SD, exp_mark[pos[nkb - 1]])], inc=VD)
            p.do("vector", lambda e: e.drain())
            p.do("vector", lambda e: e.reciprocal(rz1[:], zacc[:]), inc=VD)
            p.do("vector", lambda e: e.drain())
            cv = p.do("vector", lambda e, j0w=j0w, nk=nk, e_sb=e_sb: e.tensor_scalar(
                e_sb[:, 512 * j0w:nk], e_sb[:, 512 * j0w:nk], rz1[:], None,
                AluOpType.mult), inc=VD)
            return cv

        def p4_transp(h, t, cv_norm):
            tp = t % 2
            j0w = max(0, 128 * t - WINS[h]) // 512
            c0 = 4 * j0w
            return transpose_128(e_sb2[tp][:, 128 * c0:], t + 1 - c0,
                                 lambda j0, jn, et_sb=et_sb2[tp], c0=c0: et_sb[:, 128 * (c0 + j0):128 * (c0 + j0 + jn)],
                                 [(VD, cv_norm)], evac="alt", px2=px_med)

        def p4_pv(h, t, cse):
            nonlocal cevac_mark
            par = h % 2
            tp = t % 2
            et_sb = et_sb2[tp]
            c0 = 4 * (max(0, 128 * t - WINS[h]) // 512)
            for j2 in range(c0, t + 1):
                wt = []
                if j2 == c0:
                    wt = [(SD, max(cse[0], cevac_mark)), (VD, cse[1])]
                p.do("tensor", lambda e, j2=j2, t=t, par=par, et_sb=et_sb, c0=c0: e.matmul(
                    ps_r[:, 0:128], v_ts2[par][:, j2, :],
                    et_sb[:, 128 * j2:128 * (j2 + 1)],
                    start=(j2 == c0), stop=(j2 == t)), waits=wt, inc=TD)
            ct = p.v(TD)
            cevac_mark = p.do("scalar", lambda e, t=t, par=par: e.copy(
                ctxh_sb2[par][:, 128 * t:128 * (t + 1)], ps_r[:, 0:128]),
                waits=[(TD, ct)] + cstore_mark[par], inc=SD)

        p4_loads(0)
        lk_a = p4_alk(0)
        prev = None
        for h in range(HL):
            for t in range(16):
                jb = p4_scores(h, t, lk_a)
                cvc = p4_combine(h, t, jb)
                if prev is not None:
                    cse = p4_transp(*prev)
                p4_exp(h, t, jb, cvc)
                if prev is not None:
                    p4_pv(prev[0], prev[1], cse)
                cvn = p4_norm(h, t, jb)
                prev = (h, t, cvn)
                if t == 1 and h + 1 < HL:
                    p4_loads(h + 1)
            if h + 1 < HL:
                lk_a = p4_alk(h + 1)
            if h > 0:
                cstore_mark[(h - 1) % 2] = [dma(
                    "sync", None, cin[(h - 1) // 8][0, (h - 1) % 8],
                    ctxh_sb2[(h - 1) % 2][:], waits=[(SD, cevac_mark)])]
            if h == 9:
                cc_marks["ag2a"] = p.do("gpsimd", lambda e: e.collective_compute(
                    "AllGather", mybir.AluOpType.bypass, replica_groups=groups,
                    ins=[cin[0].opt()], outs=[cout[0].opt()]),
                    waits=fence("sync") + [(CC, p.v(CC))], inc=CC)
        cse = p4_transp(*prev)
        p4_pv(prev[0], prev[1], cse)
        cstore_mark[(HL - 1) % 2] = [dma(
            "sync", None, cin[1][0, 7], ctxh_sb2[(HL - 1) % 2][:],
            waits=[(SD, cevac_mark)])]
        p.barrier()

        cc2 = p.do("gpsimd", lambda e: e.collective_compute(
            "AllGather", mybir.AluOpType.bypass, replica_groups=groups,
            ins=[cin[1].opt()], outs=[cout[1].opt()]),
            waits=fence("sync") + [(CC, p.v(CC))], inc=CC)

        mg_done = {}
        for g in [0, 2, 1, 3]:
            r, u = g // 2, g % 2
            src = cout[u][r].rearrange("c p t -> p c t")
            wA = [(CC, cc_marks["ag2a"] if u == 0 else cc2), (VD, p.v(VD))]
            if g == 0:
                wA += [(TD, p.v(TD)), (SD, p.v(SD))]
            lA = dma("sync", None, mgA, src[:, :, 0:1024], waits=wA)
            lB = dma("sync", None, mgB, src[:, :, 1024:2048], waits=wA)
            p.do("vector", lambda e: e.tensor_tensor(
                mgB, mgB, mgA, AluOpType.subtract), waits=[lA, lB], inc=VD)
            p.do("vector", lambda e: e.drain())
            mg_done[g] = p.do("vector", lambda e, g=g: e.scalar_tensor_tensor(
                ctx_v[:, 8 * g:8 * (g + 1), :], mgB, msel_sb[:], mgA,
                AluOpType.mult, AluOpType.add), inc=VD)
            p.do("vector", lambda e: e.drain())
        iorder = list(range(0, 8)) + list(range(16, 24)) + list(range(8, 16)) + list(range(24, 32))
        cv_a = max(mg_done[0], mg_done[2])
        cv_ctx = p.v(VD)

        slot_mark2 = [p.v(TD), p.v(TD)]
        for fc in range(32):
            slot = fc % 2
            wb = w_buf[slot]
            wdma = [(TD, slot_mark2[slot])]
            if fc < 2:
                wdma.append((VD, cv_ctx))
            lw = dma("gpsimd", None, wb, wout[fc].rearrange("(i p) f -> p i f", p=128),
                     waits=wdma)
            for n_, i in enumerate(iorder):
                wt = []
                if n_ == 0:
                    wt = [lw, (VD, cv_a), (SD, p.v(SD))]
                elif n_ == 16:
                    wt = [(VD, cv_ctx)]
                for tg in range(2):
                    p.do("tensor", lambda e, i=i, tg=tg, wb=wb, n_=n_: e.matmul(
                        ps_med[:, 512 * tg:512 * (tg + 1)],
                        wb[:, i, :], ctx_v[:, i, 512 * tg:512 * (tg + 1)],
                        start=(n_ == 0), stop=(n_ == 31)),
                        waits=(wt if tg == 0 else []), inc=TD)
            slot_mark2[slot] = p.v(TD)
            ct = p.v(TD)
            cs = p.do("scalar", lambda e, fc=fc: e.activation(
                aoe_buf, ps_med[:], AF.Identity, bias=bo_sb[:, fc:fc + 1], scale=1.0),
                waits=[(TD, ct)], inc=SD)
            transpose_128(aoe_buf, 8,
                          lambda j0, jn, fc=fc: ao_v[:, j0:j0 + jn, 128 * fc:128 * (fc + 1)],
                          [(SD, cs)], evac="alt", px2=px_big3)
        p.barrier()
        for t in range(NT):
            lx = dma("sync", None, xt_p6, sc_x1[128 * t:128 * (t + 1), :],
                     waits=[(VD, p.v(VD))] + fence("sync"))
            cv = p.do("vector", lambda e, t=t: e.tensor_tensor(
                xt_p6, xt_p6, ao_v[:, t, :], AluOpType.add),
                waits=[lx], inc=VD)
            p.do("vector", lambda e: e.drain())
            dma("sync", None, sc_x2[128 * t:128 * (t + 1), :], xt_p6, waits=[(VD, cv)])
        p.barrier()

        ffn_a(sc_x2)
        p.barrier()
        ffn_b(sc_x2, out_ext, False)
        p.barrier()

        block = ctx.enter_context(nc.Block())

        def mk(name):
            def run(eng):
                for waits, fn, inc, n in p.ops[name]:
                    for s_, v_ in waits:
                        if v_ > 0:
                            eng.wait_ge(sems[s_], v_)
                    if fn is not None:
                        inst = fn(eng)
                        if inc is not None:
                            inst.then_inc(sems[inc], n)
            return run

        block.tensor(mk("tensor"))
        block.vector(mk("vector"))
        block.scalar(mk("scalar"))
        block.sync(mk("sync"))
        block.gpsimd(mk("gpsimd"))

    return nc


_NC_CACHE = {}


def _get_nc(dbg=False):
    if dbg not in _NC_CACHE:
        _NC_CACHE[dbg] = build(dbg)
    return _NC_CACHE[dbg]


def make_in_maps(L0_attn_hidden_states, L0_attn_w_qkv, L0_attn_b_qkv,
                 L0_attn_w_out, L0_attn_b_out, L0_ffn_A, L0_ffn_B, L0_ffn_Wr):
    import ml_dtypes
    x = np.ascontiguousarray(np.asarray(L0_attn_hidden_states, dtype=np.float32))
    w_qkv = np.asarray(L0_attn_w_qkv, dtype=np.float32)
    b_qkv = np.asarray(L0_attn_b_qkv, dtype=np.float32)
    w_out = np.asarray(L0_attn_w_out, dtype=np.float32)
    b_out = np.asarray(L0_attn_b_out, dtype=np.float32)
    A = np.asarray(L0_ffn_A, dtype=np.float32)
    Bm = np.asarray(L0_ffn_B, dtype=np.float32)
    Wr = np.asarray(L0_ffn_Wr, dtype=np.float32)

    slopes = get_alibi_slopes(32)
    kpos = np.arange(S, dtype=np.float32)

    a_fm = np.ascontiguousarray(A.reshape(KR, D).T)
    b_t = np.ascontiguousarray(Bm.transpose(0, 2, 1).reshape(KR, D))
    wr_fm = np.ascontiguousarray(Wr.T)
    ident = np.eye(128, dtype=np.float32)
    tri = np.where(np.arange(128)[None, :] > np.arange(128)[:, None],
                   np.float32(NEG), np.float32(0.0)).astype(np.float32)
    ones3 = np.ones((3, 128), dtype=np.float32)
    bo_col = np.ascontiguousarray(b_out.reshape(32, 128).T)

    def bf(v):
        return v.astype(ml_dtypes.bfloat16).astype(np.float32)

    perm = ([2 * i for i in range(8)] + [2 * i for i in range(8, 16)]
            + [2 * i + 1 for i in range(8)] + [2 * i + 1 for i in range(8, 16)])
    wo_p = np.ascontiguousarray(
        w_out.reshape(32, 128, D)[perm].reshape(D, D))
    wo = np.ascontiguousarray(wo_p.reshape(D, 32, 128).transpose(1, 0, 2))
    in_maps = []
    for c in range(8):
        b = c // 2
        half = c % 2
        gheads = [2 * i + half for i in range(HL)]
        wq = np.empty((3, HL, D, HD), dtype=np.float32)
        for pj in range(3):
            for h in range(HL):
                col0 = pj * D + gheads[h] * HD
                wq[pj, h] = w_qkv[:, col0:col0 + HD]
        bq_col = np.empty((128, HL), dtype=np.float32)
        bk_col = np.empty((128, HL), dtype=np.float32)
        bv_col = np.empty((128, HL), dtype=np.float32)
        for h in range(HL):
            g = gheads[h]
            bq_col[:, h] = b_qkv[0 * D + g * HD: 0 * D + g * HD + HD]
            bk_col[:, h] = b_qkv[1 * D + g * HD: 1 * D + g * HD + HD]
            bv_col[:, h] = b_qkv[2 * D + g * HD: 2 * D + g * HD + HD]
        alk3_a = np.empty((3, HL, S), dtype=np.float32)
        for h in range(HL):
            v = slopes[gheads[h]] * kpos
            hi = bf(v)
            mid = bf(v - hi)
            lo = bf(v - hi - mid)
            alk3_a[0, h], alk3_a[1, h], alk3_a[2, h] = hi, mid, lo
        in_maps.append({
            "x": np.ascontiguousarray(x[b, half * TOWN:(half + 1) * TOWN, :]),
            "wqkv": wq,
            "wout": wo,
            "a_fm": a_fm,
            "b_t": b_t,
            "wr_fm": wr_fm,
            "bq_col": bq_col,
            "bk_col": bk_col,
            "bv_col": bv_col,
            "bo_col": bo_col,
            "ident": ident,
            "tri": tri,
            "alk3": alk3_a,
            "ones3": ones3,
            "msel": np.full((128, 1), float(half), dtype=np.float32),
        })
    return in_maps


def kernel(L0_attn_hidden_states, L0_attn_w_qkv, L0_attn_b_qkv, L0_attn_w_out,
           L0_attn_b_out, L0_ffn_A, L0_ffn_B, L0_ffn_Wr):
    in_maps = make_in_maps(L0_attn_hidden_states, L0_attn_w_qkv, L0_attn_b_qkv,
                           L0_attn_w_out, L0_attn_b_out, L0_ffn_A, L0_ffn_B,
                           L0_ffn_Wr)
    nc = _get_nc(False)
    res = run_bass_kernel_spmd(nc, in_maps, core_ids=list(range(8)))
    out = np.empty((B, S, D), dtype=np.float32)
    for c in range(8):
        b = c // 2
        half = c % 2
        out[b, half * TOWN:(half + 1) * TOWN, :] = res.results[c]["out"]
    return out
